# version 17
# baseline (speedup 1.0000x reference)
import contextlib

ENGINES = ("pe", "act", "dve", "pool", "sp")
ENG_ATTR = {"pe": "tensor", "act": "scalar", "dve": "vector", "pool": "gpsimd", "sp": "sync"}

class Op:
    __slots__ = ("idx", "engine", "fn", "reads", "writes", "is_dma", "ndma",
                 "stream", "deps", "need_inc", "inc_val", "tag")
    def __init__(self, idx, engine, fn, reads, writes, is_dma, ndma, stream, tag):
        self.idx = idx
        self.engine = engine
        self.fn = fn
        self.reads = tuple(reads)
        self.writes = tuple(writes)
        self.is_dma = is_dma
        self.ndma = ndma
        self.stream = stream
        self.deps = None
        self.need_inc = bool(is_dma)
        self.inc_val = None
        self.tag = tag

class Prog:
    def __init__(self):
        self.ops = []

    def op(self, engine, fn, reads=(), writes=(), tag=""):
        o = Op(len(self.ops), engine, fn, reads, writes, False, 0, None, tag)
        self.ops.append(o)
        return o

    def dma(self, engine, stream, fn, reads=(), writes=(), ndma=1, tag=""):
        o = Op(len(self.ops), engine, fn, reads, writes, True, ndma, stream, tag)
        self.ops.append(o)
        return o

    def _compute_deps(self):
        last_write = {}
        readers = {}
        for o in self.ops:
            deps = set()
            for r in o.reads:
                w = last_write.get(r)
                if w is not None:
                    deps.add(w)
            for w in o.writes:
                lw = last_write.get(w)
                if lw is not None:
                    deps.add(lw)
                for rd in readers.get(w, ()):
                    deps.add(rd)
            deps.discard(o.idx)
            o.deps = sorted(deps)
            for r in o.reads:
                readers.setdefault(r, []).append(o.idx)
            for w in o.writes:
                last_write[w] = o.idx
                readers[w] = []

    def _mark_incs(self):
        ops = self.ops
        for o in ops:
            for d in o.deps:
                dop = ops[d]
                if dop.is_dma:
                    continue
                if dop.engine != o.engine or dop.engine != "pe":
                    dop.need_inc = True
        counters = {}
        for o in ops:
            if o.need_inc:
                key = o.stream if o.is_dma else o.engine
                step = 16 * o.ndma if o.is_dma else 1
                counters[key] = counters.get(key, 0) + step
                o.inc_val = counters[key]
        self.totals = counters

    def build(self, nc, final_engine="pool"):
        self._compute_deps()
        self._mark_incs()
        ops = self.ops
        self._stack = contextlib.ExitStack()
        sems = {}
        keys = set()
        for o in ops:
            if o.need_inc:
                keys.add(o.stream if o.is_dma else o.engine)
        for k in sorted(keys):
            sems[k] = self._stack.enter_context(nc.semaphore(f"s_{k}"))
        self.sems = sems
        dma_streams = sorted({o.stream for o in ops if o.is_dma})
        block = self._stack.enter_context(nc.Block())

        def make_stream(ename):
            eops = [o for o in ops if o.engine == ename]
            def stream(eng):
                waited = {}
                stream_issued = {}
                for o in eops:
                    need = {}
                    for d in o.deps:
                        dop = ops[d]
                        if not dop.need_inc or dop.inc_val is None:
                            continue
                        if dop.engine == o.engine and not dop.is_dma and dop.engine == "pe":
                            continue
                        key = dop.stream if dop.is_dma else dop.engine
                        if dop.inc_val > need.get(key, -1):
                            need[key] = dop.inc_val
                    if o.is_dma:
                        prev = stream_issued.get(o.stream, 0)
                        if prev > need.get(o.stream, -1):
                            need[o.stream] = prev
                    for key, v in sorted(need.items()):
                        if waited.get(key, 0) >= v:
                            continue
                        eng.wait_ge(sems[key], v)
                        waited[key] = v
                    if o.is_dma:
                        n = o.fn(eng, sems[o.stream])
                        if n != o.ndma:
                            raise RuntimeError(
                                f"op {o.idx} {o.tag}: declared ndma={o.ndma} emitted {n}")
                        stream_issued[o.stream] = o.inc_val
                    else:
                        inst = o.fn(eng)
                        if o.need_inc:
                            if inst is None:
                                raise RuntimeError(
                                    f"op {o.idx} {o.tag}: fn returned None, inc needed")
                            inst.then_inc(sems[o.engine], 1)
                if ename == final_engine:
                    for s in dma_streams:
                        tot = self.totals.get(s, 0)
                        if tot and waited.get(s, 0) < tot:
                            eng.wait_ge(sems[s], tot)
            return stream

        used = {o.engine for o in ops}
        used.add(final_engine)
        for ename in ENGINES:
            if ename in used:
                getattr(block, ENG_ATTR[ename])(make_stream(ename))

    def stats(self):
        from collections import Counter
        c = Counter(o.engine for o in self.ops)
        ninc = sum(1 for o in self.ops if o.need_inc)
        return f"ops={len(self.ops)} by-engine={dict(c)} incs={ninc}"


"""Megakernel builder for mk_prb_d1 (MLA head-mix attention + argmax-MoE, depth 1).

Per-core program: ntok tokens (rows of x) in tiles of TT=512. Fully per-token
=> pure data-parallel across cores.

Math per token:
  xn = layernorm(x); q = xn@Wq+bq; lat = xn@Wdown+bdown; kv = lat@Wup+bup
  k,v per head from kv; RoPE dropped (same-position rotation cancels in q.k)
  att[h,g] = softmax_g(q_h.k_g/8); attn = (att@v)@Wout + bout
  scores = xn@gate^T (4-term bf16 hi/lo split, fp32 accum); e* = argmax (first)
  moe = xn@We*^T via diag-masked bf16 copies, PSUM-accumulated over experts
  out = x + xn + attn + moe     (ffn includes its own +xn residual)

Key layouts (per tile of 512 tokens = 4 slices of 128):
  xbuf/xsum token-major f32 [128 x 4x1024]
  xn slice lives in SQ[s%2] f32 [128 x 1024] (reused buffer)
  xnhi bf16 token-major [128 x 4096]
  xnT / xnloT feature-major bf16 [128 x 8c*512]  (chunk c at cols c*512)
  QEO [128 x 64g*64] bf16: group g cols g*64 + c*8 + t; rows 0:63 even heads
      (head 2c d), rows 64:127 odd heads  |  XTR [64 x ...] = QEO[64:128] dup at
      partition base 0 (DRAM bounce)
  KVST [128 x 64g*128]: col g*128 + head*8 + t; rows 0:63 k(d), 64:127 v(d)
  OST / AACC bf16 feature-major [128 x 8c*512] (chunk c = heads 2c,2c+1)
  masked half-tile [128 x 8kc*256] bf16 (one per expert, rotating 2; one is
      aliased into xnloT cols 0:2048 - gate is done before P2)
  WeT (We transposed, bf16) prebuilt in DRAM as [8e, 8kc, 128, 1024]

PSUM: psA f32 [128 x 3584] banks 0-6; psB bf16 [128 x 1024] bank 7.
  P1: b0/b1 (cols 0/512) proj rotate; b2 (1024) S-quad; b3 (1536) AVaug-quad;
      b4/b5 (2048/2560) xn-transpose quads; b6 (3072) bias-T/mask/gate/scoreT;
      psB: V8-quad cols 0:256, avT-quad cols 512:1024.
  P2 (per token-half tH of 256 tokens): moe[fc] = psA cols (fc//2)*512 +
      (fc%2)*256 (banks 0-3); diag rounds in psB ([128 x 256] x 4);
      F-transposes -> the just-freed moe region.
"""
import numpy as np
import concourse.bass as bass
import concourse.mybir as mybir

F32 = mybir.dt.float32
BF16 = mybir.dt.bfloat16
AF = mybir.ActivationFunctionType
ALU = mybir.AluOpType
AX = mybir.AxisListType

D = 1024
TT = 512
NG = TT // 8
NQ = NG // 4
CSTAB = 24.0
EPS = 1e-5

def moe_ps(psA, fc):
    return psA[:, (fc // 2) * 512 + (fc % 2) * 256:(fc // 2) * 512 + (fc % 2) * 256 + 256]

def build_core(n_tiles):
    ntok = n_tiles * TT
    nc = bass.Bass("TRN2", target_bir_lowering=False)

    x_d = nc.dram_tensor("x", [ntok, D], F32, kind="ExternalInput").ap()
    Wq_d = nc.dram_tensor("Wq", [D, D], F32, kind="ExternalInput").ap()
    bq_d = nc.dram_tensor("bq", [D], F32, kind="ExternalInput").ap()
    Wdown_d = nc.dram_tensor("Wdown", [D, 256], F32, kind="ExternalInput").ap()
    bdown_d = nc.dram_tensor("bdown", [256], F32, kind="ExternalInput").ap()
    Wup_d = nc.dram_tensor("Wup", [256, 2048], F32, kind="ExternalInput").ap()
    bup_d = nc.dram_tensor("bup", [2048], F32, kind="ExternalInput").ap()
    Wout_d = nc.dram_tensor("Wout", [D, D], F32, kind="ExternalInput").ap()
    bout_d = nc.dram_tensor("bout", [D], F32, kind="ExternalInput").ap()
    gw_d = nc.dram_tensor("gw", [8, D], F32, kind="ExternalInput").ap()
    ew_d = nc.dram_tensor("ew", [8, D, D], F32, kind="ExternalInput").ap()
    out_d = nc.dram_tensor("out", [ntok, D], F32, kind="ExternalOutput").ap()
    WeB_d = nc.dram_tensor("WeB", [8, D, D], BF16, kind="Internal").ap()
    WeT_d = nc.dram_tensor("WeT", [8, 8, 128, 1024], BF16, kind="Internal").ap()
    qdup_d = nc.dram_tensor("qdup", [2, 64, NG * 64], BF16, kind="Internal").ap()
    idx4_d = nc.dram_tensor("idx4", [2, 512], BF16, kind="Internal").ap()

    from contextlib import ExitStack
    ctx = ExitStack()
    E = ctx.enter_context
    WqS = E(nc.sbuf_tensor("WqS", [128, 8 * 1024], BF16))
    WdownS = E(nc.sbuf_tensor("WdownS", [128, 8 * 256], BF16))
    WupS = E(nc.sbuf_tensor("WupS", [128, 2 * 2048], BF16))
    WoutS = E(nc.sbuf_tensor("WoutS", [128, 8 * 1024], BF16))
    slot = [E(nc.sbuf_tensor(f"slot{r}", [128, 4 * 1024], BF16)) for r in range(3)]
    xbuf = E(nc.sbuf_tensor("xbuf", [128, 4096], F32))
    xr = E(nc.sbuf_tensor("xr", [128, 2048], F32))
    STT2 = E(nc.sbuf_tensor("STT2", [128, 16], F32))
    xnTb = [E(nc.sbuf_tensor(f"xnTb{r}", [128, 4096], BF16)) for r in range(2)]
    maskedA = E(nc.sbuf_tensor("maskedA", [128, 2048], BF16))
    maskedB = E(nc.sbuf_tensor("maskedB", [128, 2048], BF16))
    MK = E(nc.sbuf_tensor("MK", [128, 4096], BF16))
    IDXR = E(nc.sbuf_tensor("IDXR", [1, 512], BF16))
    ONES1 = E(nc.sbuf_tensor("ONES1", [1, 128], BF16))
    SQ_ = E(nc.sbuf_tensor("SQ", [128, 1024], F32)); SQ = [SQ_, SQ_]
    QEO = E(nc.sbuf_tensor("QEO", [128, NG * 64], BF16))
    XTR = E(nc.sbuf_tensor("XTR", [64, NG * 64], BF16))
    KVST = E(nc.sbuf_tensor("KVST", [128, NG * 128], BF16))
    latT = E(nc.sbuf_tensor("latT", [128, 2 * 512], BF16))
    OST = E(nc.sbuf_tensor("OST", [128, 8 * 512], BF16, is_transpose_dest=True))
    AACC = E(nc.sbuf_tensor("AACC", [128, 8 * 512], BF16, is_transpose_dest=True))
    xnloT = E(nc.sbuf_tensor("xnloT", [128, 4096], BF16))
    Xq = [E(nc.sbuf_tensor(f"Xq{r}", [128, 512], BF16)) for r in range(2)]
    V8a = [E(nc.sbuf_tensor(f"V8a{r}", [128, 4 * 65], BF16)) for r in range(2)]
    AVS = [E(nc.sbuf_tensor(f"AVS{r}", [128, 4 * 64], BF16)) for r in range(3)]
    rDv = [E(nc.sbuf_tensor(f"rD{r}", [128, 4], F32)) for r in range(2)]
    Fv = [E(nc.sbuf_tensor(f"Fv{r}", [128, 512], F32)) for r in range(2)]
    Rm = E(nc.sbuf_tensor("Rm", [8, 128], BF16))
    mask4 = E(nc.sbuf_tensor("mask4", [128, 512], BF16))
    EMq = [E(nc.sbuf_tensor(f"EMq{r}", [128, 512], BF16)) for r in range(2)]
    I8 = E(nc.sbuf_tensor("I8", [8, 8], BF16))
    identb = E(nc.sbuf_tensor("identb", [128, 128], BF16))
    identf = E(nc.sbuf_tensor("identf", [128, 128], F32))
    cpool = E(nc.sbuf_tensor("cpool", [128, 8], F32))
    onesf = E(nc.sbuf_tensor("onesf", [1, 1], F32))
    BR = E(nc.sbuf_tensor("BR", [34, 128], F32))
    BCOL = E(nc.sbuf_tensor("BCOL", [128, 34], F32))
    GT = E(nc.sbuf_tensor("GT", [128, 8 * 16], BF16))
    Grow = xr[0:8, 0:1024]
    Ghi = AACC[0:8, 0:1024]
    Glo = AACC[0:8, 1024:2048]
    SCf = E(nc.sbuf_tensor("SCf", [16, 512], F32))
    scraw = E(nc.sbuf_tensor("scraw", [128, 64], F32))
    sctmp = E(nc.sbuf_tensor("sctmp", [128, 48], F32))
    sctmp2 = E(nc.sbuf_tensor("sctmp2", [128, 32], F32))
    sctmp3 = E(nc.sbuf_tensor("sctmp3", [128, 4], F32))
    idxu = E(nc.sbuf_tensor("idxu", [128, 32], mybir.dt.uint32))
    STT = E(nc.sbuf_tensor("STT", [128, 32], F32))
    psA = E(nc.psum_tensor("psA", [128, 3072], F32))
    psV = E(nc.psum_tensor("psV", [128, 1024], BF16))
    psB = E(nc.psum_tensor("psB", [128, 1024], BF16))

    EXPBIAS = -24.0 - (14.125 * 14.125) / 8.0
    CONSTS = {0.125: 0, -CSTAB: 1, -0.5: 2, 1.0: 3, EXPBIAS: 4}
    for cv, idx in CONSTS.items():
        nc.const_aps.register(F32, cv, cpool[:, idx:idx+1])

    P = Prog()

    def c_mem1(g):
        for cv, idx in CONSTS.items():
            g.memset(cpool[:, idx:idx+1], cv)
        g.memset(onesf[:, :], 1.0)
        g.memset(ONES1[:, :], 1.0)
        g.memset(I8[:, :], 0.0)
        g.memset(identb[:, :], 0.0)
        g.memset(identf[:, :], 0.0)
        for r in range(2):
            g.memset(V8a[r][:, :].rearrange("p (g c) -> p g c", c=65)[:, :, 64:65], 1.0)
        return g.memset(IDXR[:, :], 0.0)
    P.op("pool", c_mem1, writes=["k0"], tag="mem1")
    def c_mem2(g):
        g.affine_select(out=I8[:, :], in_=I8[:, :], compare_op=ALU.not_equal,
                        fill=1.0, base=0, pattern=[[-1, 8]], channel_multiplier=1)
        g.affine_select(out=identb[:, :], in_=identb[:, :], compare_op=ALU.not_equal,
                        fill=1.0, base=0, pattern=[[-1, 128]], channel_multiplier=1)
        return g.affine_select(out=identf[:, :], in_=identf[:, :], compare_op=ALU.not_equal,
                               fill=1.0, base=0, pattern=[[-1, 128]], channel_multiplier=1)
    P.op("pool", c_mem2, reads=["k0"], writes=["consts"], tag="mem2")

    def c_srm(sy, sem):
        for j in range(16):
            sy.dma_start(Rm[:, 8*j:8*(j+1)], I8[:, :]).then_inc(sem, 16)
        sy.dma_start(Grow, gw_d).then_inc(sem, 16)
        sy.dma_start(BR[0:8, :], bq_d.rearrange("(a b) -> a b", b=128)).then_inc(sem, 16)
        sy.dma_start(BR[8:10, :], bdown_d.rearrange("(a b) -> a b", b=128)).then_inc(sem, 16)
        sy.dma_start(BR[10:26, :], bup_d.rearrange("(a b) -> a b", b=128)).then_inc(sem, 16)
        sy.dma_start(BR[26:34, :], bout_d.rearrange("(a b) -> a b", b=128)).then_inc(sem, 16)
        return 21
    P.dma("sp", "srm", c_srm, reads=["consts", "ostg"], writes=["Rm", "ostg", "BR", "xrbuf"], ndma=21, tag="srm")

    def c_wres(g, sem):
        for kc in range(8):
            g.dma_start(WqS[:, kc*1024:(kc+1)*1024], Wq_d[kc*128:(kc+1)*128, :]).then_inc(sem, 16)
        for kc in range(8):
            g.dma_start(WdownS[:, kc*256:(kc+1)*256], Wdown_d[kc*128:(kc+1)*128, :]).then_inc(sem, 16)
        for kc in range(2):
            g.dma_start(WupS[:, kc*2048:(kc+1)*2048], Wup_d[kc*128:(kc+1)*128, :]).then_inc(sem, 16)
        for kc in range(8):
            g.dma_start(WoutS[:, kc*1024:(kc+1)*1024], Wout_d[kc*128:(kc+1)*128, :]).then_inc(sem, 16)
        return 26
    P.dma("pool", "wcast", c_wres, writes=["WqS", "WdownS", "WupS", "WoutS"], ndma=26, tag="wres")
    def mk_xin(i):
        def f(g, sem):
            src = x_d[i*TT:(i+1)*TT, :].rearrange("(s p) f -> p s f", p=128)
            dst = xbuf[:, :].rearrange("p (s f) -> p s f", f=1024)
            g.dma_start(dst, src).then_inc(sem, 16)
            return 1
        return f
    P.dma("pool", "xin", mk_xin(0), writes=["x"], tag="xin0")

    P.op("dve", lambda v: v.tensor_copy(Ghi, Grow),
         reads=["ostg", "AACC"], writes=["AACC"], tag="ghi")
    P.op("dve", lambda v: v.tensor_tensor(Glo, Grow, Ghi, ALU.subtract),
         reads=["ostg", "xrbuf", "AACC"], writes=["AACC"], tag="glo")
    def c_gt(t):
        last = None
        for kc in range(8):
            last = t.transpose(psV[:, kc*16:kc*16+8], Ghi[:, kc*128:(kc+1)*128], identb[0:8, 0:8])
            last = t.transpose(psV[:, kc*16+8:kc*16+16], Glo[:, kc*128:(kc+1)*128], identb[0:8, 0:8])

        return last
    P.op("pe", c_gt, reads=["AACC", "consts", "psBv0"], writes=["psBv0"], tag="gtT")
    P.op("act", lambda s: s.copy(GT[:, :], psV[:, 0:128]),
         reads=["psBv0"], writes=["GT"], tag="gtev")

    for e in range(8):
        P.dma("pool", "wcast",
              (lambda g, sem, e=e: (g.dma_start(WeB_d[e], ew_d[e]).then_inc(sem, 16), 1)[1]),
              writes=[f"WeB{e}"], tag=f"webcast{e}")

    for e in range(8):
        strm = f"wprep{e % 2}"
        qoff = (e % 2) * 2048
        for rnd in range(4):
            def c_xp(sy, sem, e=e, rnd=rnd, qoff=qoff):
                for k in range(2):
                    c = rnd*2 + k
                    dst = AACC[:, qoff + k*1024: qoff + k*1024 + 1024]
                    sy.dma_start(dst, WeB_d[e][:, c*128:(c+1)*128], transpose=True).then_inc(sem, 16)
                return 2
            P.dma("sp", strm, c_xp, reads=[f"WeB{e}", f"wst{e%2}", "AACC"], writes=[f"wstw{e%2}", "AACC"],
                  ndma=2, tag=f"wxp{e}_{rnd}")
            def c_st(sy, sem, e=e, rnd=rnd, qoff=qoff):
                for k in range(2):
                    c = rnd*2 + k
                    src = AACC[:, qoff + k*1024: qoff + k*1024 + 1024]
                    sy.dma_start(WeT_d[e, c], src).then_inc(sem, 16)
                return 2
            P.dma("sp", strm, c_st, reads=[f"wstw{e%2}", "AACC"], writes=[f"wst{e%2}", f"WeT{e}", "AACC"],
                  ndma=2, tag=f"wst{e}_{rnd}")

    P.op("pe", lambda t: t.transpose(psA[:, 2048:2082], BR[0:34, :], identf[0:34, 0:34]),
         reads=["BR", "consts", "b4"], writes=["b4"], tag="biasmm")
    P.op("dve", lambda v: v.tensor_copy(BCOL[:, :], psA[:, 2048:2082]),
         reads=["b4"], writes=["BCOL"], tag="bcol")


    P.op("pe", lambda t: t.matmul(psA[:, 2048:2176], Rm[:, :], Rm[:, :], start=True, stop=True),
         reads=["Rm", "b4", "BCOL"], writes=["b4"], tag="maskmm")
    def c_mask4(v):
        last = None
        for j in range(4):
            last = v.tensor_copy(mask4[:, j*128:(j+1)*128], psA[:, 2048:2176])
        return last
    P.op("dve", c_mask4, reads=["b4"], writes=["mask4"], tag="mask4")


    def emit_ln(ti):
        pl = ti % 2
        xnTcl = xnTb[pl]
        xkl = f"xnT{pl}"
        for s in range(4):
            sr = s % 2
            xs = xbuf[:, s*1024:(s+1)*1024]
            st = STT[:, s*8:s*8+8]
            P.op("dve", (lambda v, xs=xs, st=st: v.reduce_sum(st[:, 0:1], xs, axis=AX.X)),
                 reads=["x", f"st{s}"], writes=[f"st{s}"], tag=f"t{ti}sum{s}")
            P.op("act", (lambda a, xs=xs, sr=sr: a.activation(SQ[sr][:, :], xs, AF.Square)),
                 reads=["x", "SQk"], writes=["SQk"], tag=f"t{ti}sq{s}")
            P.op("dve", (lambda v, st=st, sr=sr: v.reduce_sum(st[:, 1:2], SQ[sr][:, :], axis=AX.X)),
                 reads=["SQk", f"st{s}"], writes=[f"st{s}"], tag=f"t{ti}sqs{s}")
            P.op("dve", (lambda v, st=st: v.tensor_scalar(st[:, 2:3], st[:, 0:1], 1.0 / D, None, ALU.mult)),
                 reads=[f"st{s}"], writes=[f"st{s}"], tag=f"t{ti}mean{s}")
            P.op("dve", (lambda v, st=st: v.tensor_scalar(st[:, 5:6], st[:, 2:3], st[:, 2:3], EPS,
                                                          ALU.mult, ALU.subtract)),
                 reads=[f"st{s}"], writes=[f"st{s}"], tag=f"t{ti}msq{s}")
            P.op("dve", (lambda v, st=st: v.scalar_tensor_tensor(st[:, 6:7], st[:, 1:2], 1.0 / D,
                                                                 st[:, 5:6], ALU.mult, ALU.subtract)),
                 reads=[f"st{s}"], writes=[f"st{s}"], tag=f"t{ti}var{s}")
            P.op("act", (lambda a, st=st: a.activation(st[:, 7:8], st[:, 6:7], AF.Ln)),
                 reads=[f"st{s}", "consts"], writes=[f"st{s}"], tag=f"t{ti}ln{s}")
            P.op("act", (lambda a, st=st: a.activation(st[:, 7:8], st[:, 7:8], AF.Exp, scale=-0.5)),
                 reads=[f"st{s}", "consts"], writes=[f"st{s}"], tag=f"t{ti}rstd{s}")
            def c_norm(v, xs=xs, st=st, sr=sr):
                return v.tensor_scalar(SQ[sr][:, :], xs, st[:, 2:3], st[:, 7:8],
                                       ALU.subtract, ALU.mult)
            P.op("dve", c_norm, reads=["x", f"st{s}", "SQk"], writes=["SQk"], tag=f"t{ti}norm{s}")
            def c_ab(v, st=st, s=s, par=pl):
                b0 = par*8 + s*2
                v.tensor_scalar(STT2[:, b0:b0+1], st[:, 7:8], 1.0, None, ALU.add)
                return v.tensor_scalar(STT2[:, b0+1:b0+2], st[:, 2:3], st[:, 7:8], -1.0,
                                       ALU.mult, ALU.mult)
            P.op("dve", c_ab, reads=[f"st{s}", f"stt2{pl}"], writes=[f"stt2{pl}"], tag=f"t{ti}ab{s}")
            def c_xnt(t, sr=sr):
                last = None
                for c in range(8):
                    po = 2048 + (c // 4) * 512 + (c % 4) * 128
                    last = t.transpose(psA[:, po:po+128], SQ[sr][:, c*128:(c+1)*128], identf[:, :])
                return last
            P.op("pe", c_xnt, reads=["SQk", "consts", "b4", "b5"],
                 writes=["b4", "b5"], tag=f"t{ti}xnt{s}")
            for half in range(2):
                po = 2048 + half * 512
                bk = "b4" if half == 0 else "b5"
                def c_hev(a, s=s, half=half, po=po, xnTc=xnTcl):
                    dst = xnTc[:, :].rearrange("p (c x) -> p c x", x=512)[:, half*4:(half+1)*4,
                                                                        s*128:(s+1)*128]
                    return a.copy(dst, psA[:, po:po+512].rearrange("p (c x) -> p c x", x=128))
                P.op("act", c_hev, reads=[bk, xkl], writes=[xkl], tag=f"t{ti}hev{s}{half}")
                def c_lev(v, s=s, half=half, po=po, xnTc=xnTcl):
                    dst = xnloT[:, :].rearrange("p (c x) -> p c x", x=512)[:, half*4:(half+1)*4,
                                                                          s*128:(s+1)*128]
                    srcT = xnTc[:, :].rearrange("p (c x) -> p c x", x=512)[:, half*4:(half+1)*4,
                                                                         s*128:(s+1)*128]
                    return v.tensor_tensor(dst, psA[:, po:po+512].rearrange("p (c x) -> p c x", x=128),
                                           srcT, ALU.subtract)
                P.op("dve", c_lev, reads=[bk, xkl, "xnloTh0", "xnloTh1"],
                     writes=["xnloTh0", "xnloTh1"], tag=f"t{ti}lev{s}{half}")

    emit_ln(0)
    for i in range(n_tiles):
        par = i % 2
        xnTc = xnTb[par]
        xkey = f"xnT{par}"


        for mc in range(8):
            bank = ["b0", "b1"][mc % 2]
            po = [0, 512][mc % 2]
            def c_wq(t, mc=mc, po=po, xnTc=xnTc):
                last = None
                for kc in range(8):
                    last = t.matmul(psA[:, po:po+512], WqS[:, kc*1024+mc*128:kc*1024+(mc+1)*128],
                                    xnTc[:, kc*512:(kc+1)*512], start=(kc == 0), stop=(kc == 7))
                return last
            P.op("pe", c_wq, reads=["WqS", xkey, bank], writes=[bank], tag=f"t{i}wq{mc}")
            def c_qev(v, mc=mc, po=po):
                dst = QEO[:, :].rearrange("p (g x) -> p g x", x=64)[:, :, mc*8:(mc+1)*8]
                return v.tensor_scalar(dst, psA[:, po:po+512].rearrange("p (g t) -> p g t", t=8),
                                       BCOL[:, mc:mc+1], None, ALU.add)
            P.op("dve", c_qev, reads=[bank, "BCOL", "QEO"], writes=["QEO"], tag=f"t{i}qev{mc}")
        P.dma("pool", "qd",
              (lambda g, sem, par=par: (g.dma_start(qdup_d[par], QEO[64:128, :]).then_inc(sem, 16), 1)[1]),
              reads=["QEO"], writes=[f"qdram{par}"], tag=f"t{i}qdst")
        P.dma("pool", "qd",
              (lambda g, sem, par=par: (g.dma_start(XTR[:, :], qdup_d[par]).then_inc(sem, 16), 1)[1]),
              reads=[f"qdram{par}", "XTR"], writes=["XTR"], tag=f"t{i}qdld")

        for mc in range(2):
            bank = ["b0", "b1"][mc]
            po = [0, 512][mc]
            def c_wd(t, mc=mc, po=po, xnTc=xnTc):
                last = None
                for kc in range(8):
                    last = t.matmul(psA[:, po:po+512], WdownS[:, kc*256+mc*128:kc*256+(mc+1)*128],
                                    xnTc[:, kc*512:(kc+1)*512], start=(kc == 0), stop=(kc == 7))
                return last
            P.op("pe", c_wd, reads=["WdownS", xkey, bank], writes=[bank], tag=f"t{i}wd{mc}")
            P.op("dve", (lambda v, mc=mc, po=po: v.tensor_scalar(
                    latT[:, mc*512:(mc+1)*512], psA[:, po:po+512], BCOL[:, 8+mc:9+mc], None, ALU.add)),
                 reads=[bank, "BCOL", "latT"], writes=["latT"], tag=f"t{i}dev{mc}")

        for c in range(16):
            bank = ["b0", "b1"][c % 2]
            po = [0, 512][c % 2]
            def c_kv(t, c=c, po=po):
                last = None
                for kc in range(2):
                    last = t.matmul(psA[:, po:po+512], WupS[:, kc*2048+c*128:kc*2048+(c+1)*128],
                                    latT[:, kc*512:(kc+1)*512], start=(kc == 0), stop=(kc == 1))
                return last
            P.op("pe", c_kv, reads=["WupS", "latT", bank], writes=[bank], tag=f"t{i}kv{c}")
            def c_kvev(a, c=c, po=po):
                dst = KVST[:, :].rearrange("p (g x) -> p g x", x=128)[:, :, c*8:(c+1)*8]
                return a.activation(dst, psA[:, po:po+512].rearrange("p (g t) -> p g t", t=8),
                                    AF.Identity, bias=BCOL[:, 10+c:11+c])
            P.op("act", c_kvev, reads=[bank, "BCOL", "consts", "KVST"], writes=["KVST"], tag=f"t{i}kvev{c}")

        def c_gate(t, xnTc=xnTc):
            last = None
            for p2, rhs in ((0, xnTc), (1, xnloT)):
                for kc in range(8):
                    last = t.matmul(psA[0:16, 2048:2560], GT[:, kc*16:(kc+1)*16],
                                    rhs[:, kc*512:(kc+1)*512],
                                    start=(p2 == 0 and kc == 0), stop=(p2 == 1 and kc == 7))
            return last
        P.op("pe", c_gate, reads=["GT", xkey, "xnloTh0", "xnloTh1", "b4"], writes=["b4"], tag=f"t{i}gate")
        P.op("dve", lambda v: v.tensor_copy(SCf[:, :], psA[0:16, 2048:2560]),
             reads=["b4", "SCf"], writes=["SCf"], tag=f"t{i}scf")
        def c_sct(t):
            last = None
            for s in range(4):
                last = t.transpose(psA[:, 2048+s*16:2048+(s+1)*16], SCf[:, s*128:(s+1)*128],
                                   identf[0:16, 0:16])
            return last
        P.op("pe", c_sct, reads=["SCf", "consts", "b4"], writes=["b4"], tag=f"t{i}sct")
        P.op("dve", lambda v: v.tensor_copy(scraw[:, :], psA[:, 2048:2112]),
             reads=["b4", "scraw"], writes=["scraw"], tag=f"t{i}scraw")
        def c_oha(v):
            last = None
            for s in range(4):
                last = v.tensor_tensor(sctmp[:, s*12:s*12+8], scraw[:, s*16:s*16+8],
                                       scraw[:, s*16+8:s*16+16], ALU.add)
            return last
        P.op("dve", c_oha, reads=["scraw", "sctmp"], writes=["sctmp"], tag=f"t{i}oha")
        def c_ohb(v):
            last = None
            for s in range(4):
                last = v.max(sctmp2[:, s*8:(s+1)*8], sctmp[:, s*12:s*12+8])
            return last
        P.op("dve", c_ohb, reads=["sctmp", "sctmp2"], writes=["sctmp2"], tag=f"t{i}ohb")
        def c_ohc(v):
            last = None
            for s in range(4):
                last = v.max_index(idxu[:, s*8:(s+1)*8], sctmp2[:, s*8:(s+1)*8], sctmp[:, s*12:s*12+8])
            return last
        P.op("dve", c_ohc, reads=["sctmp", "sctmp2", "idxu"], writes=["idxu"], tag=f"t{i}ohc")
        def c_ohd(v):
            last = None
            for s in range(4):
                last = v.tensor_copy(sctmp3[:, s:s+1], idxu[:, s*8:s*8+1])
            return last
        P.op("dve", c_ohd, reads=["idxu", "sctmp3"], writes=["sctmp3"], tag=f"t{i}ohd")
        P.op("pe", lambda t: t.transpose(psA[0:4, 2560:2688], sctmp3[:, 0:4], identf[:, :]),
             reads=["sctmp3", "consts", "b5"], writes=["b5"], tag=f"t{i}idxt")
        P.op("dve", lambda v: v.tensor_copy(SCf[0:4, 0:128], psA[0:4, 2560:2688]),
             reads=["b5", "idx4"], writes=["idx4"], tag=f"t{i}idxe")
        def c_ist(g, sem, par=par):
            g.dma_start(idx4_d[par].rearrange("(s c) -> s c", s=4), SCf[0:4, 0:128]).then_inc(sem, 16)
            return 1
        P.dma("pool", "mkd", c_ist, reads=["idx4"], writes=[f"idxd{par}"], tag=f"t{i}ist")
        if i + 1 < n_tiles:
            P.dma("pool", "xin", mk_xin(i + 1), reads=["x"], writes=["x"], tag=f"xin{i+1}")

        def emit_head(q):
            r = q % 2
            base = q * 4
            spo = 1024
            sbank = "b2"
            vpo = 0
            def c_smm(t, base=base, spo=spo):
                last = None
                for gg in range(4):
                    gi = base + gg
                    last = t.matmul(psA[:, spo+gg*128:spo+gg*128+64],
                                    KVST[0:64, gi*128:(gi+1)*128],
                                    QEO[0:64, gi*64:(gi+1)*64], start=True, stop=True)
                    last = t.matmul(psA[:, spo+gg*128+64:spo+gg*128+128],
                                    KVST[0:64, gi*128:(gi+1)*128],
                                    XTR[0:64, gi*64:(gi+1)*64], start=True, stop=True)
                return last
            P.op("pe", c_smm, reads=["KVST", "QEO", "XTR", sbank], writes=[sbank], tag=f"t{i}smm{q}")
            P.op("act", (lambda a, r=r, spo=spo: a.activation(Xq[r][:, :], psA[:, spo:spo+512], AF.Exp,
                                                     bias=-CSTAB, scale=0.125)),
                 reads=[sbank, "consts", f"Xq{r}"], writes=[f"Xq{r}", sbank], tag=f"t{i}exp{q}")
            def c_v8t(t, base=base, vpo=vpo):
                last = None
                for gg in range(4):
                    gi = base + gg
                    last = t.transpose(psV[:, vpo+gg*64:vpo+(gg+1)*64],
                                       KVST[64:128, gi*128:(gi+1)*128], identb[64:128, 64:128])
                return last
            P.op("pe", c_v8t, reads=["KVST", "consts", "psBv0"], writes=["psBv0"], tag=f"t{i}v8t{q}")
            def c_v8e(v, r=r, vpo=vpo):
                return v.tensor_copy(V8a[r][:, :].rearrange("p (g c) -> p g c", c=65)[:, :, 0:64],
                                     psV[:, vpo:vpo+256].rearrange("p (g c) -> p g c", c=64))
            P.op("dve", c_v8e, reads=["psBv0", f"V8a{r}"], writes=[f"V8a{r}"], tag=f"t{i}v8e{q}")
        def emit_em(q):
            r = q % 2
            P.op("dve", (lambda v, r=r: v.tensor_tensor(
                    EMq[r][:, :], Xq[r][:, :], mask4[:, :], ALU.mult)),
                 reads=[f"Xq{r}", "mask4", f"EMq{r}"], writes=[f"EMq{r}"], tag=f"t{i}em{q}")
        def emit_tailA(q):
            r = q % 2
            ra = q % 3
            apo = 1536
            abank = "b3"
            def c_av(t, r=r, apo=apo):
                last = None
                for gg in range(4):
                    last = t.matmul(psA[:, apo+gg*65:apo+(gg+1)*65], EMq[r][:, gg*128:(gg+1)*128],
                                    V8a[r][:, gg*65:(gg+1)*65], start=True, stop=True)
                return last
            P.op("pe", c_av, reads=[f"EMq{r}", f"V8a{r}", abank], writes=[abank], tag=f"t{i}av{q}")
            P.op("dve", (lambda v, r=r, apo=apo: v.reciprocal(
                    rDv[r][:, :], psA[:, apo:apo+260].rearrange("p (g c) -> p g c", c=65)[:, :, 64:65])),
                 reads=[abank, f"rD{r}"], writes=[f"rD{r}"], tag=f"t{i}rd{q}")
            def c_avs(v, r=r, ra=ra, apo=apo):
                last = None
                for gg in range(4):
                    last = v.tensor_scalar(AVS[ra][:, gg*64:(gg+1)*64], psA[:, apo+gg*65:apo+gg*65+64],
                                           rDv[r][:, gg:gg+1], None, ALU.mult)
                return last
            P.op("dve", c_avs, reads=[abank, f"rD{r}", f"AVS{ra}"], writes=[f"AVS{ra}"], tag=f"t{i}avs{q}")
        def emit_avt(q):
            ra = q % 3
            def c_avt(t, ra=ra):
                last = None
                for gg in range(4):
                    last = t.transpose(psB[0:64, gg*64:(gg+1)*64],
                                       AVS[ra][0:64, gg*64:(gg+1)*64], identb[0:64, 0:64])
                    last = t.transpose(psB[64:128, gg*64:(gg+1)*64],
                                       AVS[ra][64:128, gg*64:(gg+1)*64], identb[64:128, 64:128])
                return last
            P.op("pe", c_avt, reads=[f"AVS{ra}", "consts", "psBa"], writes=["psBa"], tag=f"t{i}avt{q}")
        def emit_ost(q):
            base = q * 4
            def c_ost(a, base=base):
                last = None
                for gg in range(4):
                    gi = base + gg
                    srcE = psB[0:64, gg*64:(gg+1)*64].rearrange("p (c t) -> p c t", t=8)
                    dstE = OST[0:64, :].rearrange("p (c x) -> p c x", x=512)[:, :, gi*8:(gi+1)*8]
                    a.copy(dstE, srcE)
                    srcO = psB[64:128, gg*64:(gg+1)*64].rearrange("p (c t) -> p c t", t=8)
                    dstO = OST[64:128, :].rearrange("p (c x) -> p c x", x=512)[:, :, gi*8:(gi+1)*8]
                    last = a.copy(dstO, srcO)
                return last
            P.op("act", c_ost, reads=["psBa", "OST"], writes=["OST"], tag=f"t{i}ost{q}")
        for n in range(NQ + 3):
            if 0 <= n - 1 < NQ:
                emit_tailA(n - 1)
            if n < NQ:
                emit_head(n)
                emit_em(n)
            if 0 <= n - 3:
                emit_avt(n - 3)
                emit_ost(n - 3)

        def c_ild(g, sem, par=par):
            g.dma_start(IDXR[0:1, :], idx4_d[par].rearrange("(o f) -> o f", o=1)).then_inc(sem, 16)
            return 1
        P.dma("pool", "mkd", c_ild, reads=[f"idxd{par}", "IDXR"], writes=["IDXR"], tag=f"t{i}ild")
        P.op("pe", lambda t: t.matmul(psA[:, 2048:2560], ONES1[0:1, :], IDXR[0:1, :],
                                      start=True, stop=True),
             reads=["consts", "IDXR", "b4"], writes=["b4"], tag=f"t{i}idxb")
        for e in range(8):
            def c_mke(v, e=e):
                return v.tensor_scalar(MK[:, e*512:(e+1)*512], psA[:, 2048:2560], float(e),
                                       None, ALU.is_equal)
            P.op("dve", c_mke, reads=["b4", "MK"], writes=["MK"], tag=f"t{i}mke{e}")
        for e in range(2):
            pmbuf = (maskedA, maskedB)[e % 2]
            pmkey = ("maskedA", "maskedB")[e % 2]
            for hf in range(2):
                def c_mskp(v, e=e, hf=hf, pmbuf=pmbuf, xnTc=xnTc):
                    last = None
                    for k in range(4):
                        kc = hf*4 + k
                        last = v.tensor_tensor(
                            pmbuf[:, kc*256:(kc+1)*256],
                            xnTc[:, kc*512:kc*512+256],
                            MK[:, e*512:e*512+256], ALU.mult)
                    return last
                P.op("dve", c_mskp, reads=[xkey, "MK", f"{pmkey}h{hf}"],
                     writes=[f"{pmkey}h{hf}"], tag=f"t{i}msk0_{e}_{hf}")
        if i + 1 < n_tiles:
            emit_ln(i + 1)

        for mc in range(8):
            bank = ["b0", "b1"][mc % 2]
            po = [0, 512][mc % 2]
            def c_wo(t, mc=mc, po=po):
                last = None
                for kc in range(8):
                    last = t.matmul(psA[:, po:po+512], WoutS[:, kc*1024+mc*128:kc*1024+(mc+1)*128],
                                    OST[:, kc*512:(kc+1)*512], start=(kc == 0), stop=(kc == 7))
                return last
            P.op("pe", c_wo, reads=["WoutS", "OST", bank], writes=[bank], tag=f"t{i}wo{mc}")
            P.op("act", (lambda a, mc=mc, po=po: a.activation(
                    AACC[:, mc*512:(mc+1)*512], psA[:, po:po+512], AF.Identity,
                    bias=BCOL[:, 26+mc:27+mc])),
                 reads=[bank, "BCOL", "consts", "AACC"], writes=["AACC"], tag=f"t{i}woev{mc}")

        def emit_p2_part(part, i=i, par=par, xnTc=xnTc, xkey=xkey):
            half_seq = [(tH, e, hf) for tH in range(2) for e in range(8) for hf in range(2)]
            def mk_wload(e, hf, r):
                def f(sy, sem):
                    src = WeT_d[e, hf*4:(hf+1)*4].rearrange("k p f -> p k f")
                    dst = slot[r][:, :].rearrange("p (k f) -> p k f", f=1024)
                    sy.dma_start(dst, src).then_inc(sem, 16)
                    return 1
                return f
            if part == 0:
                for j in range(3):
                    tH, e, hf = half_seq[j]
                    P.dma("sp", f"wexp{j % 3}", mk_wload(e, hf, j % 3), reads=[f"WeT{e}", f"slot{j%3}"],
                          writes=[f"slot{j%3}"], tag=f"t{i}wl{j}")
            tH = 0 if part < 2 else 1
            erange = range(0, 5) if part % 2 == 0 else range(5, 9)
            if True:
                mk = ("maskedA", "maskedB")
                mbufs = (maskedA, maskedB)
                def emit_prep(e, hfs, tH=tH):
                    rm = e % 2
                    mbuf = mbufs[rm]
                    mkey = mk[rm]
                    for hf in hfs:
                        def c_msk(v, e=e, hf=hf, tH=tH, mbuf=mbuf, xnTc=xnTc):
                            last = None
                            for k in range(4):
                                kc = hf*4 + k
                                last = v.tensor_tensor(
                                    mbuf[:, kc*256:(kc+1)*256],
                                    xnTc[:, kc*512+tH*256:kc*512+tH*256+256],
                                    MK[:, e*512+tH*256:e*512+tH*256+256], ALU.mult)
                            return last
                        P.op("dve", c_msk, reads=[xkey, "MK", f"{mkey}h{hf}"],
                             writes=[f"{mkey}h{hf}"], tag=f"t{i}msk{tH}_{e}_{hf}")
                def emit_emm(e, hfs, tH=tH):
                    rm = e % 2
                    mbuf = mbufs[rm]
                    mkey = mk[rm]
                    for hf in hfs:
                        j = tH*16 + e*2 + hf
                        for fh in range(2):
                            moe_keys = [f"moe{fc}" for fc in range(fh*4, fh*4+4)]
                            def c_emm(t, e=e, hf=hf, j=j, tH=tH, mbuf=mbuf, fh=fh):
                                last = None
                                for k in range(4):
                                    kc = hf*4 + k
                                    for fc in range(fh*4, fh*4+4):
                                        last = t.matmul(moe_ps(psA, fc),
                                                        slot[j % 3][:, k*1024+fc*128:k*1024+(fc+1)*128],
                                                        mbuf[:, kc*256:(kc+1)*256],
                                                        start=(e == 0 and kc == 0 and fc % 2 == 0),
                                                        stop=(e == 7 and kc == 7),
                                                        skip_group_check=True)
                                return last
                            bank_w = (["b0", "b1"] if fh == 0 else ["b2", "b3"]) \
                                if (tH == 0 and e == 0 and hf == 0) else []
                            P.op("pe", c_emm, reads=[f"slot{j%3}", f"{mkey}h{hf}"] + moe_keys + bank_w,
                                 writes=moe_keys + bank_w, tag=f"t{i}emm{tH}_{e}_{hf}_{fh}")
                        jn = j + 3
                        if jn < 32:
                            tHn, en, hfn = half_seq[jn]
                            P.dma("sp", f"wexp{jn % 3}", mk_wload(en, hfn, jn % 3),
                                  reads=[f"WeT{en}", f"slot{jn%3}"], writes=[f"slot{jn%3}"],
                                  tag=f"t{i}wl{jn}")
                for e in erange:
                    skip_pre = (part == 0 and e < 2)
                    if e < 8 and not skip_pre:
                        emit_prep(e, (0,))
                    if e >= 1:
                        emit_emm(e - 1, (0,))
                    if e < 8 and not skip_pre:
                        emit_prep(e, (1,))
                    if e >= 1:
                        emit_emm(e - 1, (1,))
                if part % 2 == 0:
                    return
                def c_xrl(g, sem, i=i, tH=tH):
                    src = x_d[i*TT + tH*256:i*TT + (tH+1)*256, :].rearrange("(s p) f -> p s f", p=128)
                    dst = xr[:, :].rearrange("p (s f) -> p s f", f=1024)
                    g.dma_start(dst, src).then_inc(sem, 16)
                    return 1
                P.dma("pool", "xr", c_xrl, reads=["xrbuf"], writes=["xrbuf"], tag=f"t{i}xrl{tH}")
                for h in range(2):
                    s = tH*2 + h
                    def c_xab(v, h=h, s=s, par=par):
                        b0 = par*8 + s*2
                        return v.tensor_scalar(xr[:, h*1024:(h+1)*1024], xr[:, h*1024:(h+1)*1024],
                                               STT2[:, b0:b0+1], STT2[:, b0+1:b0+2],
                                               ALU.mult, ALU.add)
                    P.op("dve", c_xab, reads=["xrbuf", f"stt2{par}"], writes=["xrbuf"], tag=f"t{i}xab{tH}{h}")
                for fp in range(4):
                    fc = fp * 2
                    r = fp % 2
                    mkeys2 = [f"moe{fc}", f"moe{fc+1}"]
                    def c_fcomb(v, fc=fc, tH=tH, r=r):
                        asrc = AACC[:, :].rearrange("p (c x) -> p c x", x=512)[:, fc:fc+2,
                                                                              tH*256:(tH+1)*256]
                        return v.tensor_tensor(Fv[r][:, :].rearrange("p (c x) -> p c x", x=256),
                                               psA[:, fc*256:fc*256+512].rearrange(
                                                   "p (c x) -> p c x", x=256),
                                               asrc, ALU.add)
                    P.op("dve", c_fcomb, reads=mkeys2 + ["AACC", f"Fv{r}"],
                         writes=[f"Fv{r}"], tag=f"t{i}fc{tH}_{fp}")
                    def c_ft(t, fc=fc, r=r):
                        last = None
                        for q2 in range(4):
                            last = t.transpose(psA[:, fc*256+q2*128:fc*256+(q2+1)*128],
                                               Fv[r][:, q2*128:(q2+1)*128], identf[:, :])
                        return last
                    P.op("pe", c_ft, reads=[f"Fv{r}", "consts"] + mkeys2,
                         writes=mkeys2, tag=f"t{i}ft{tH}_{fp}")
                    for fc2 in (fc, fc + 1):
                        def c_fin(v, fc=fc2, tH=tH):
                            src = moe_ps(psA, fc).rearrange("p (h f) -> p h f", f=128)
                            xsv = xr[:, :].rearrange("p (s f) -> p s f", f=1024)[:, 0:2,
                                                                                fc*128:(fc+1)*128]
                            return v.tensor_tensor(xsv, src, xsv, ALU.add)
                        P.op("dve", c_fin, reads=[f"moe{fc2}", "xrbuf"],
                             writes=["xrbuf"], tag=f"t{i}fin{tH}_{fc2}")
                for h in range(2):
                    s = tH*2 + h
                    def c_st(g, sem, i=i, s=s, h=h):
                        g.dma_start(out_d[i*TT + s*128:i*TT + (s+1)*128, :],
                                    xr[:, h*1024:(h+1)*1024]).then_inc(sem, 16)
                        return 1
                    P.dma("pool", "xout", c_st, reads=["xrbuf"], writes=["outd"], tag=f"t{i}st{s}")
        for _pp in range(4):
            emit_p2_part(_pp)


    P.build(nc)
    nc._ctx_stack = ctx
    return nc, P

_CACHED = {}

def _get_nc():
    if "nc" not in _CACHED:
        nc, P = build_core(16)
        _CACHED["nc"] = nc
    return _CACHED["nc"]

def kernel(L0_attn_x, L0_attn_Wq, L0_attn_bq, L0_attn_Wdown, L0_attn_bdown,
           L0_attn_Wup, L0_attn_bup, L0_attn_Wout, L0_attn_bout,
           L0_ffn_gate_w, L0_ffn_expert_w):
    from concourse.bass_utils import run_bass_kernel_spmd
    x = np.ascontiguousarray(np.asarray(L0_attn_x, dtype=np.float32).reshape(65536, 1024))
    shared = {
        "Wq": np.ascontiguousarray(np.asarray(L0_attn_Wq, dtype=np.float32)),
        "bq": np.ascontiguousarray(np.asarray(L0_attn_bq, dtype=np.float32)),
        "Wdown": np.ascontiguousarray(np.asarray(L0_attn_Wdown, dtype=np.float32)),
        "bdown": np.ascontiguousarray(np.asarray(L0_attn_bdown, dtype=np.float32)),
        "Wup": np.ascontiguousarray(np.asarray(L0_attn_Wup, dtype=np.float32)),
        "bup": np.ascontiguousarray(np.asarray(L0_attn_bup, dtype=np.float32)),
        "Wout": np.ascontiguousarray(np.asarray(L0_attn_Wout, dtype=np.float32)),
        "bout": np.ascontiguousarray(np.asarray(L0_attn_bout, dtype=np.float32)),
        "gw": np.ascontiguousarray(np.asarray(L0_ffn_gate_w, dtype=np.float32)),
        "ew": np.ascontiguousarray(np.asarray(L0_ffn_expert_w, dtype=np.float32)),
    }
    in_maps = []
    for c in range(8):
        m = dict(shared)
        m["x"] = np.ascontiguousarray(x[c*8192:(c+1)*8192])
        in_maps.append(m)
    nc = _get_nc()
    res = run_bass_kernel_spmd(nc, in_maps, core_ids=list(range(8)))
    out = np.concatenate([res.results[c]["out"] for c in range(8)], axis=0)
    return out.reshape(16, 4096, 1024)


# version 18
# speedup vs baseline: 1.0577x; 1.0577x over previous
import contextlib

ENGINES = ("pe", "act", "dve", "pool", "sp")
ENG_ATTR = {"pe": "tensor", "act": "scalar", "dve": "vector", "pool": "gpsimd", "sp": "sync"}

class Op:
    __slots__ = ("idx", "engine", "fn", "reads", "writes", "is_dma", "ndma",
                 "stream", "deps", "need_inc", "inc_val", "tag")
    def __init__(self, idx, engine, fn, reads, writes, is_dma, ndma, stream, tag):
        self.idx = idx
        self.engine = engine
        self.fn = fn
        self.reads = tuple(reads)
        self.writes = tuple(writes)
        self.is_dma = is_dma
        self.ndma = ndma
        self.stream = stream
        self.deps = None
        self.need_inc = bool(is_dma)
        self.inc_val = None
        self.tag = tag

class Prog:
    def __init__(self):
        self.ops = []

    def op(self, engine, fn, reads=(), writes=(), tag=""):
        o = Op(len(self.ops), engine, fn, reads, writes, False, 0, None, tag)
        self.ops.append(o)
        return o

    def dma(self, engine, stream, fn, reads=(), writes=(), ndma=1, tag=""):
        o = Op(len(self.ops), engine, fn, reads, writes, True, ndma, stream, tag)
        self.ops.append(o)
        return o

    def _compute_deps(self):
        last_write = {}
        readers = {}
        for o in self.ops:
            deps = set()
            for r in o.reads:
                w = last_write.get(r)
                if w is not None:
                    deps.add(w)
            for w in o.writes:
                lw = last_write.get(w)
                if lw is not None:
                    deps.add(lw)
                for rd in readers.get(w, ()):
                    deps.add(rd)
            deps.discard(o.idx)
            o.deps = sorted(deps)
            for r in o.reads:
                readers.setdefault(r, []).append(o.idx)
            for w in o.writes:
                last_write[w] = o.idx
                readers[w] = []

    def _mark_incs(self):
        ops = self.ops
        for o in ops:
            for d in o.deps:
                dop = ops[d]
                if dop.is_dma:
                    continue
                if dop.engine != o.engine or dop.engine != "pe":
                    dop.need_inc = True
        counters = {}
        for o in ops:
            if o.need_inc:
                key = o.stream if o.is_dma else o.engine
                step = 16 * o.ndma if o.is_dma else 1
                counters[key] = counters.get(key, 0) + step
                o.inc_val = counters[key]
        self.totals = counters

    def build(self, nc, final_engine="pool"):
        self._compute_deps()
        self._mark_incs()
        ops = self.ops
        self._stack = contextlib.ExitStack()
        sems = {}
        keys = set()
        for o in ops:
            if o.need_inc:
                keys.add(o.stream if o.is_dma else o.engine)
        for k in sorted(keys):
            sems[k] = self._stack.enter_context(nc.semaphore(f"s_{k}"))
        self.sems = sems
        dma_streams = sorted({o.stream for o in ops if o.is_dma})
        block = self._stack.enter_context(nc.Block())

        def make_stream(ename):
            eops = [o for o in ops if o.engine == ename]
            def stream(eng):
                waited = {}
                stream_issued = {}
                for o in eops:
                    need = {}
                    for d in o.deps:
                        dop = ops[d]
                        if not dop.need_inc or dop.inc_val is None:
                            continue
                        if dop.engine == o.engine and not dop.is_dma and dop.engine == "pe":
                            continue
                        key = dop.stream if dop.is_dma else dop.engine
                        if dop.inc_val > need.get(key, -1):
                            need[key] = dop.inc_val
                    if o.is_dma:
                        prev = stream_issued.get(o.stream, 0)
                        if prev > need.get(o.stream, -1):
                            need[o.stream] = prev
                    for key, v in sorted(need.items()):
                        if waited.get(key, 0) >= v:
                            continue
                        eng.wait_ge(sems[key], v)
                        waited[key] = v
                    if o.is_dma:
                        n = o.fn(eng, sems[o.stream])
                        if n != o.ndma:
                            raise RuntimeError(
                                f"op {o.idx} {o.tag}: declared ndma={o.ndma} emitted {n}")
                        stream_issued[o.stream] = o.inc_val
                    else:
                        inst = o.fn(eng)
                        if o.need_inc:
                            if inst is None:
                                raise RuntimeError(
                                    f"op {o.idx} {o.tag}: fn returned None, inc needed")
                            inst.then_inc(sems[o.engine], 1)
                if ename == final_engine:
                    for s in dma_streams:
                        tot = self.totals.get(s, 0)
                        if tot and waited.get(s, 0) < tot:
                            eng.wait_ge(sems[s], tot)
            return stream

        used = {o.engine for o in ops}
        used.add(final_engine)
        for ename in ENGINES:
            if ename in used:
                getattr(block, ENG_ATTR[ename])(make_stream(ename))

    def stats(self):
        from collections import Counter
        c = Counter(o.engine for o in self.ops)
        ninc = sum(1 for o in self.ops if o.need_inc)
        return f"ops={len(self.ops)} by-engine={dict(c)} incs={ninc}"


"""Megakernel builder for mk_prb_d1 (MLA head-mix attention + argmax-MoE, depth 1).

Per-core program: ntok tokens (rows of x) in tiles of TT=512. Fully per-token
=> pure data-parallel across cores.

Math per token:
  xn = layernorm(x); q = xn@Wq+bq; lat = xn@Wdown+bdown; kv = lat@Wup+bup
  k,v per head from kv; RoPE dropped (same-position rotation cancels in q.k)
  att[h,g] = softmax_g(q_h.k_g/8); attn = (att@v)@Wout + bout
  scores = xn@gate^T (4-term bf16 hi/lo split, fp32 accum); e* = argmax (first)
  moe = xn@We*^T via diag-masked bf16 copies, PSUM-accumulated over experts
  out = x + xn + attn + moe     (ffn includes its own +xn residual)

Key layouts (per tile of 512 tokens = 4 slices of 128):
  xbuf/xsum token-major f32 [128 x 4x1024]
  xn slice lives in SQ[s%2] f32 [128 x 1024] (reused buffer)
  xnhi bf16 token-major [128 x 4096]
  xnT / xnloT feature-major bf16 [128 x 8c*512]  (chunk c at cols c*512)
  QEO [128 x 64g*64] bf16: group g cols g*64 + c*8 + t; rows 0:63 even heads
      (head 2c d), rows 64:127 odd heads  |  XTR [64 x ...] = QEO[64:128] dup at
      partition base 0 (DRAM bounce)
  KVST [128 x 64g*128]: col g*128 + head*8 + t; rows 0:63 k(d), 64:127 v(d)
  OST / AACC bf16 feature-major [128 x 8c*512] (chunk c = heads 2c,2c+1)
  masked half-tile [128 x 8kc*256] bf16 (one per expert, rotating 2; one is
      aliased into xnloT cols 0:2048 - gate is done before P2)
  WeT (We transposed, bf16) prebuilt in DRAM as [8e, 8kc, 128, 1024]

PSUM: psA f32 [128 x 3584] banks 0-6; psB bf16 [128 x 1024] bank 7.
  P1: b0/b1 (cols 0/512) proj rotate; b2 (1024) S-quad; b3 (1536) AVaug-quad;
      b4/b5 (2048/2560) xn-transpose quads; b6 (3072) bias-T/mask/gate/scoreT;
      psB: V8-quad cols 0:256, avT-quad cols 512:1024.
  P2 (per token-half tH of 256 tokens): moe[fc] = psA cols (fc//2)*512 +
      (fc%2)*256 (banks 0-3); diag rounds in psB ([128 x 256] x 4);
      F-transposes -> the just-freed moe region.
"""
import numpy as np
import concourse.bass as bass
import concourse.mybir as mybir

F32 = mybir.dt.float32
BF16 = mybir.dt.bfloat16
AF = mybir.ActivationFunctionType
ALU = mybir.AluOpType
AX = mybir.AxisListType

D = 1024
TT = 512
NG = TT // 8
NQ = NG // 4
CSTAB = 24.0
EPS = 1e-5

def moe_ps(psA, fc):
    return psA[:, (fc // 2) * 512 + (fc % 2) * 256:(fc // 2) * 512 + (fc % 2) * 256 + 256]

def build_core(n_tiles):
    ntok = n_tiles * TT
    nc = bass.Bass("TRN2", target_bir_lowering=False)

    x_d = nc.dram_tensor("x", [ntok, D], F32, kind="ExternalInput").ap()
    Wq_d = nc.dram_tensor("Wq", [D, D], F32, kind="ExternalInput").ap()
    bq_d = nc.dram_tensor("bq", [D], F32, kind="ExternalInput").ap()
    Wdown_d = nc.dram_tensor("Wdown", [D, 256], F32, kind="ExternalInput").ap()
    bdown_d = nc.dram_tensor("bdown", [256], F32, kind="ExternalInput").ap()
    Wup_d = nc.dram_tensor("Wup", [256, 2048], F32, kind="ExternalInput").ap()
    bup_d = nc.dram_tensor("bup", [2048], F32, kind="ExternalInput").ap()
    Wout_d = nc.dram_tensor("Wout", [D, D], F32, kind="ExternalInput").ap()
    bout_d = nc.dram_tensor("bout", [D], F32, kind="ExternalInput").ap()
    gw_d = nc.dram_tensor("gw", [8, D], F32, kind="ExternalInput").ap()
    ew_d = nc.dram_tensor("ew", [8, D, D], F32, kind="ExternalInput").ap()
    out_d = nc.dram_tensor("out", [ntok, D], F32, kind="ExternalOutput").ap()
    WeB_d = nc.dram_tensor("WeB", [8, D, D], BF16, kind="Internal").ap()
    WeT_d = nc.dram_tensor("WeT", [8, 8, 128, 1024], BF16, kind="Internal").ap()
    qdup_d = nc.dram_tensor("qdup", [2, 64, NG * 64], BF16, kind="Internal").ap()
    idx4_d = nc.dram_tensor("idx4", [2, 512], BF16, kind="Internal").ap()

    from contextlib import ExitStack
    ctx = ExitStack()
    E = ctx.enter_context
    WqS = E(nc.sbuf_tensor("WqS", [128, 8 * 1024], BF16))
    WdownS = E(nc.sbuf_tensor("WdownS", [128, 8 * 256], BF16))
    WupS = E(nc.sbuf_tensor("WupS", [128, 2 * 2048], BF16))
    WoutS = E(nc.sbuf_tensor("WoutS", [128, 8 * 1024], BF16))
    slot = [E(nc.sbuf_tensor(f"slot{r}", [128, 4 * 1024], BF16)) for r in range(3)]
    xbuf = E(nc.sbuf_tensor("xbuf", [128, 4096], F32))
    xr = E(nc.sbuf_tensor("xr", [128, 2048], F32))
    STT2 = E(nc.sbuf_tensor("STT2", [128, 16], F32))
    xnTb = [E(nc.sbuf_tensor(f"xnTb{r}", [128, 4096], BF16)) for r in range(2)]
    maskedA = E(nc.sbuf_tensor("maskedA", [128, 2048], BF16))
    maskedB = E(nc.sbuf_tensor("maskedB", [128, 2048], BF16))
    MK = E(nc.sbuf_tensor("MK", [128, 4096], BF16))
    IDXR = E(nc.sbuf_tensor("IDXR", [1, 512], BF16))
    ONES1 = E(nc.sbuf_tensor("ONES1", [1, 128], BF16))
    SQ_ = E(nc.sbuf_tensor("SQ", [128, 1024], F32)); SQ = [SQ_, SQ_]
    QEO = E(nc.sbuf_tensor("QEO", [128, NG * 64], BF16))
    XTR = E(nc.sbuf_tensor("XTR", [64, NG * 64], BF16))
    KVST = E(nc.sbuf_tensor("KVST", [128, NG * 128], BF16))
    latT = E(nc.sbuf_tensor("latT", [128, 2 * 512], BF16))
    OST = E(nc.sbuf_tensor("OST", [128, 8 * 512], BF16, is_transpose_dest=True))
    AACC = E(nc.sbuf_tensor("AACC", [128, 8 * 512], BF16, is_transpose_dest=True))
    xnloT = E(nc.sbuf_tensor("xnloT", [128, 4096], BF16))
    Xq = [E(nc.sbuf_tensor(f"Xq{r}", [128, 512], BF16)) for r in range(2)]
    V8a = [E(nc.sbuf_tensor(f"V8a{r}", [128, 4 * 65], BF16)) for r in range(2)]
    AVS = [E(nc.sbuf_tensor(f"AVS{r}", [128, 4 * 64], BF16)) for r in range(3)]
    rDv = [E(nc.sbuf_tensor(f"rD{r}", [128, 4], F32)) for r in range(2)]
    Fv = [E(nc.sbuf_tensor(f"Fv{r}", [128, 512], F32)) for r in range(2)]
    Rm = E(nc.sbuf_tensor("Rm", [8, 128], BF16))
    mask4 = E(nc.sbuf_tensor("mask4", [128, 512], BF16))
    EMq = [E(nc.sbuf_tensor(f"EMq{r}", [128, 512], BF16)) for r in range(2)]
    I8 = E(nc.sbuf_tensor("I8", [8, 8], BF16))
    identb = E(nc.sbuf_tensor("identb", [128, 128], BF16))
    identf = E(nc.sbuf_tensor("identf", [128, 128], F32))
    cpool = E(nc.sbuf_tensor("cpool", [128, 8], F32))
    onesf = E(nc.sbuf_tensor("onesf", [1, 1], F32))
    BR = E(nc.sbuf_tensor("BR", [34, 128], F32))
    BCOL = E(nc.sbuf_tensor("BCOL", [128, 34], F32))
    GT = E(nc.sbuf_tensor("GT", [128, 8 * 16], BF16))
    Grow = xr[0:8, 0:1024]
    Ghi = AACC[0:8, 0:1024]
    Glo = AACC[0:8, 1024:2048]
    SCf = E(nc.sbuf_tensor("SCf", [16, 512], F32))
    scraw = E(nc.sbuf_tensor("scraw", [128, 64], F32))
    sctmp = E(nc.sbuf_tensor("sctmp", [128, 48], F32))
    sctmp2 = E(nc.sbuf_tensor("sctmp2", [128, 32], F32))
    sctmp3 = E(nc.sbuf_tensor("sctmp3", [128, 4], F32))
    idxu = E(nc.sbuf_tensor("idxu", [128, 32], mybir.dt.uint32))
    STT = E(nc.sbuf_tensor("STT", [128, 32], F32))
    psA = E(nc.psum_tensor("psA", [128, 3072], F32))
    psV = E(nc.psum_tensor("psV", [128, 1024], BF16))
    psB = E(nc.psum_tensor("psB", [128, 1024], BF16))

    EXPBIAS = -24.0 - (14.125 * 14.125) / 8.0
    CONSTS = {0.125: 0, -CSTAB: 1, -0.5: 2, 1.0: 3, EXPBIAS: 4}
    for cv, idx in CONSTS.items():
        nc.const_aps.register(F32, cv, cpool[:, idx:idx+1])

    P = Prog()

    def c_mem1(g):
        for cv, idx in CONSTS.items():
            g.memset(cpool[:, idx:idx+1], cv)
        g.memset(onesf[:, :], 1.0)
        g.memset(ONES1[:, :], 1.0)
        g.memset(I8[:, :], 0.0)
        g.memset(identb[:, :], 0.0)
        g.memset(identf[:, :], 0.0)
        for r in range(2):
            g.memset(V8a[r][:, :].rearrange("p (g c) -> p g c", c=65)[:, :, 64:65], 1.0)
        return g.memset(IDXR[:, :], 0.0)
    P.op("pool", c_mem1, writes=["k0"], tag="mem1")
    def c_mem2(g):
        g.affine_select(out=I8[:, :], in_=I8[:, :], compare_op=ALU.not_equal,
                        fill=1.0, base=0, pattern=[[-1, 8]], channel_multiplier=1)
        g.affine_select(out=identb[:, :], in_=identb[:, :], compare_op=ALU.not_equal,
                        fill=1.0, base=0, pattern=[[-1, 128]], channel_multiplier=1)
        return g.affine_select(out=identf[:, :], in_=identf[:, :], compare_op=ALU.not_equal,
                               fill=1.0, base=0, pattern=[[-1, 128]], channel_multiplier=1)
    P.op("pool", c_mem2, reads=["k0"], writes=["consts"], tag="mem2")

    def c_srm(sy, sem):
        for j in range(16):
            sy.dma_start(Rm[:, 8*j:8*(j+1)], I8[:, :]).then_inc(sem, 16)
        sy.dma_start(Grow, gw_d).then_inc(sem, 16)
        sy.dma_start(BR[0:8, :], bq_d.rearrange("(a b) -> a b", b=128)).then_inc(sem, 16)
        sy.dma_start(BR[8:10, :], bdown_d.rearrange("(a b) -> a b", b=128)).then_inc(sem, 16)
        sy.dma_start(BR[10:26, :], bup_d.rearrange("(a b) -> a b", b=128)).then_inc(sem, 16)
        sy.dma_start(BR[26:34, :], bout_d.rearrange("(a b) -> a b", b=128)).then_inc(sem, 16)
        return 21
    P.dma("sp", "srm", c_srm, reads=["consts", "ostg"], writes=["Rm", "ostg", "BR", "xrbuf"], ndma=21, tag="srm")

    def c_wres(g, sem):
        for kc in range(8):
            g.dma_start(WqS[:, kc*1024:(kc+1)*1024], Wq_d[kc*128:(kc+1)*128, :]).then_inc(sem, 16)
        for kc in range(8):
            g.dma_start(WdownS[:, kc*256:(kc+1)*256], Wdown_d[kc*128:(kc+1)*128, :]).then_inc(sem, 16)
        for kc in range(2):
            g.dma_start(WupS[:, kc*2048:(kc+1)*2048], Wup_d[kc*128:(kc+1)*128, :]).then_inc(sem, 16)
        for kc in range(8):
            g.dma_start(WoutS[:, kc*1024:(kc+1)*1024], Wout_d[kc*128:(kc+1)*128, :]).then_inc(sem, 16)
        return 26
    P.dma("pool", "wcast", c_wres, writes=["WqS", "WdownS", "WupS", "WoutS"], ndma=26, tag="wres")
    def mk_xin(i):
        def f(g, sem):
            src = x_d[i*TT:(i+1)*TT, :].rearrange("(s p) f -> p s f", p=128)
            dst = xbuf[:, :].rearrange("p (s f) -> p s f", f=1024)
            g.dma_start(dst, src).then_inc(sem, 16)
            return 1
        return f
    P.dma("pool", "xin", mk_xin(0), writes=["x"], tag="xin0")

    P.op("dve", lambda v: v.tensor_copy(Ghi, Grow),
         reads=["ostg", "AACC"], writes=["AACC"], tag="ghi")
    P.op("dve", lambda v: v.tensor_tensor(Glo, Grow, Ghi, ALU.subtract),
         reads=["ostg", "xrbuf", "AACC"], writes=["AACC"], tag="glo")
    def c_gt(t):
        last = None
        for kc in range(8):
            last = t.transpose(psV[:, kc*16:kc*16+8], Ghi[:, kc*128:(kc+1)*128], identb[0:8, 0:8])
            last = t.transpose(psV[:, kc*16+8:kc*16+16], Glo[:, kc*128:(kc+1)*128], identb[0:8, 0:8])

        return last
    P.op("pe", c_gt, reads=["AACC", "consts", "psBv0"], writes=["psBv0"], tag="gtT")
    P.op("act", lambda s: s.copy(GT[:, :], psV[:, 0:128]),
         reads=["psBv0"], writes=["GT"], tag="gtev")

    for e in range(8):
        P.dma("pool", "wcast",
              (lambda g, sem, e=e: (g.dma_start(WeB_d[e], ew_d[e]).then_inc(sem, 16), 1)[1]),
              writes=[f"WeB{e}"], tag=f"webcast{e}")

    for e in range(8):
        strm = f"wprep{e % 2}"
        qoff = (e % 2) * 2048
        for rnd in range(4):
            def c_xp(sy, sem, e=e, rnd=rnd, qoff=qoff):
                for k in range(2):
                    c = rnd*2 + k
                    dst = AACC[:, qoff + k*1024: qoff + k*1024 + 1024]
                    sy.dma_start(dst, WeB_d[e][:, c*128:(c+1)*128], transpose=True).then_inc(sem, 16)
                return 2
            P.dma("sp", strm, c_xp, reads=[f"WeB{e}", f"wst{e%2}", "AACC"], writes=[f"wstw{e%2}", "AACC"],
                  ndma=2, tag=f"wxp{e}_{rnd}")
            def c_st(sy, sem, e=e, rnd=rnd, qoff=qoff):
                for k in range(2):
                    c = rnd*2 + k
                    src = AACC[:, qoff + k*1024: qoff + k*1024 + 1024]
                    sy.dma_start(WeT_d[e, c], src).then_inc(sem, 16)
                return 2
            P.dma("sp", strm, c_st, reads=[f"wstw{e%2}", "AACC"], writes=[f"wst{e%2}", f"WeT{e}", "AACC"],
                  ndma=2, tag=f"wst{e}_{rnd}")

    P.op("pe", lambda t: t.transpose(psA[:, 2048:2082], BR[0:34, :], identf[0:34, 0:34]),
         reads=["BR", "consts", "b4"], writes=["b4"], tag="biasmm")
    P.op("dve", lambda v: v.tensor_copy(BCOL[:, :], psA[:, 2048:2082]),
         reads=["b4"], writes=["BCOL"], tag="bcol")


    P.op("pe", lambda t: t.matmul(psA[:, 2048:2176], Rm[:, :], Rm[:, :], start=True, stop=True),
         reads=["Rm", "b4", "BCOL"], writes=["b4"], tag="maskmm")
    def c_mask4(v):
        last = None
        for j in range(4):
            last = v.tensor_copy(mask4[:, j*128:(j+1)*128], psA[:, 2048:2176])
        return last
    P.op("dve", c_mask4, reads=["b4"], writes=["mask4"], tag="mask4")


    pend_p2 = []
    for i in range(n_tiles):
        par = i % 2
        xnTc = xnTb[par]
        xkey = f"xnT{par}"
        for s in range(4):
            if pend_p2:
                pend_p2.pop(0)()
            sr = s % 2
            xs = xbuf[:, s*1024:(s+1)*1024]
            st = STT[:, s*8:s*8+8]
            P.op("dve", (lambda v, xs=xs, st=st: v.reduce_sum(st[:, 0:1], xs, axis=AX.X)),
                 reads=["x", f"st{s}"], writes=[f"st{s}"], tag=f"t{i}sum{s}")
            P.op("act", (lambda a, xs=xs, sr=sr: a.activation(SQ[sr][:, :], xs, AF.Square)),
                 reads=["x", "SQk"], writes=["SQk"], tag=f"t{i}sq{s}")
            P.op("dve", (lambda v, st=st, sr=sr: v.reduce_sum(st[:, 1:2], SQ[sr][:, :], axis=AX.X)),
                 reads=["SQk", f"st{s}"], writes=[f"st{s}"], tag=f"t{i}sqs{s}")
            P.op("dve", (lambda v, st=st: v.tensor_scalar(st[:, 2:3], st[:, 0:1], 1.0 / D, None, ALU.mult)),
                 reads=[f"st{s}"], writes=[f"st{s}"], tag=f"t{i}mean{s}")
            P.op("dve", (lambda v, st=st: v.tensor_scalar(st[:, 5:6], st[:, 2:3], st[:, 2:3], EPS,
                                                          ALU.mult, ALU.subtract)),
                 reads=[f"st{s}"], writes=[f"st{s}"], tag=f"t{i}msq{s}")
            P.op("dve", (lambda v, st=st: v.scalar_tensor_tensor(st[:, 6:7], st[:, 1:2], 1.0 / D,
                                                                 st[:, 5:6], ALU.mult, ALU.subtract)),
                 reads=[f"st{s}"], writes=[f"st{s}"], tag=f"t{i}var{s}")
            P.op("act", (lambda a, st=st: a.activation(st[:, 7:8], st[:, 6:7], AF.Ln)),
                 reads=[f"st{s}", "consts"], writes=[f"st{s}"], tag=f"t{i}ln{s}")
            P.op("act", (lambda a, st=st: a.activation(st[:, 7:8], st[:, 7:8], AF.Exp, scale=-0.5)),
                 reads=[f"st{s}", "consts"], writes=[f"st{s}"], tag=f"t{i}rstd{s}")
            def c_norm(v, xs=xs, st=st, sr=sr):
                return v.tensor_scalar(SQ[sr][:, :], xs, st[:, 2:3], st[:, 7:8],
                                       ALU.subtract, ALU.mult)
            P.op("dve", c_norm, reads=["x", f"st{s}", "SQk"], writes=["SQk"], tag=f"t{i}norm{s}")
            def c_ab(v, st=st, s=s, par=par):
                b0 = par*8 + s*2
                v.tensor_scalar(STT2[:, b0:b0+1], st[:, 7:8], 1.0, None, ALU.add)
                return v.tensor_scalar(STT2[:, b0+1:b0+2], st[:, 2:3], st[:, 7:8], -1.0,
                                       ALU.mult, ALU.mult)
            P.op("dve", c_ab, reads=[f"st{s}", f"stt2{par}"], writes=[f"stt2{par}"], tag=f"t{i}ab{s}")
            def c_xnt(t, sr=sr):
                last = None
                for c in range(8):
                    po = 2048 + (c // 4) * 512 + (c % 4) * 128
                    last = t.transpose(psA[:, po:po+128], SQ[sr][:, c*128:(c+1)*128], identf[:, :])
                return last
            P.op("pe", c_xnt, reads=["SQk", "consts", "b4", "b5"],
                 writes=["b4", "b5"], tag=f"t{i}xnt{s}")
            for half in range(2):
                po = 2048 + half * 512
                bk = "b4" if half == 0 else "b5"
                def c_hev(a, s=s, half=half, po=po, xnTc=xnTc):
                    dst = xnTc[:, :].rearrange("p (c x) -> p c x", x=512)[:, half*4:(half+1)*4,
                                                                        s*128:(s+1)*128]
                    return a.copy(dst, psA[:, po:po+512].rearrange("p (c x) -> p c x", x=128))
                P.op("act", c_hev, reads=[bk, xkey], writes=[xkey], tag=f"t{i}hev{s}{half}")
                def c_lev(v, s=s, half=half, po=po, xnTc=xnTc):
                    dst = xnloT[:, :].rearrange("p (c x) -> p c x", x=512)[:, half*4:(half+1)*4,
                                                                          s*128:(s+1)*128]
                    srcT = xnTc[:, :].rearrange("p (c x) -> p c x", x=512)[:, half*4:(half+1)*4,
                                                                         s*128:(s+1)*128]
                    return v.tensor_tensor(dst, psA[:, po:po+512].rearrange("p (c x) -> p c x", x=128),
                                           srcT, ALU.subtract)
                P.op("dve", c_lev, reads=[bk, xkey, "xnloTh0", "xnloTh1"],
                     writes=["xnloTh0", "xnloTh1"], tag=f"t{i}lev{s}{half}")
        while pend_p2:
            pend_p2.pop(0)()



        for mc in range(8):
            bank = ["b0", "b1"][mc % 2]
            po = [0, 512][mc % 2]
            def c_wq(t, mc=mc, po=po, xnTc=xnTc):
                last = None
                for kc in range(8):
                    last = t.matmul(psA[:, po:po+512], WqS[:, kc*1024+mc*128:kc*1024+(mc+1)*128],
                                    xnTc[:, kc*512:(kc+1)*512], start=(kc == 0), stop=(kc == 7))
                return last
            P.op("pe", c_wq, reads=["WqS", xkey, bank], writes=[bank], tag=f"t{i}wq{mc}")
            def c_qev(v, mc=mc, po=po):
                dst = QEO[:, :].rearrange("p (g x) -> p g x", x=64)[:, :, mc*8:(mc+1)*8]
                return v.tensor_scalar(dst, psA[:, po:po+512].rearrange("p (g t) -> p g t", t=8),
                                       BCOL[:, mc:mc+1], None, ALU.add)
            P.op("dve", c_qev, reads=[bank, "BCOL", "QEO"], writes=["QEO"], tag=f"t{i}qev{mc}")
        P.dma("pool", "qd",
              (lambda g, sem, par=par: (g.dma_start(qdup_d[par], QEO[64:128, :]).then_inc(sem, 16), 1)[1]),
              reads=["QEO"], writes=[f"qdram{par}"], tag=f"t{i}qdst")
        P.dma("pool", "qd",
              (lambda g, sem, par=par: (g.dma_start(XTR[:, :], qdup_d[par]).then_inc(sem, 16), 1)[1]),
              reads=[f"qdram{par}", "XTR"], writes=["XTR"], tag=f"t{i}qdld")

        for mc in range(2):
            bank = ["b0", "b1"][mc]
            po = [0, 512][mc]
            def c_wd(t, mc=mc, po=po, xnTc=xnTc):
                last = None
                for kc in range(8):
                    last = t.matmul(psA[:, po:po+512], WdownS[:, kc*256+mc*128:kc*256+(mc+1)*128],
                                    xnTc[:, kc*512:(kc+1)*512], start=(kc == 0), stop=(kc == 7))
                return last
            P.op("pe", c_wd, reads=["WdownS", xkey, bank], writes=[bank], tag=f"t{i}wd{mc}")
            P.op("dve", (lambda v, mc=mc, po=po: v.tensor_scalar(
                    latT[:, mc*512:(mc+1)*512], psA[:, po:po+512], BCOL[:, 8+mc:9+mc], None, ALU.add)),
                 reads=[bank, "BCOL", "latT"], writes=["latT"], tag=f"t{i}dev{mc}")

        for c in range(16):
            bank = ["b0", "b1"][c % 2]
            po = [0, 512][c % 2]
            def c_kv(t, c=c, po=po):
                last = None
                for kc in range(2):
                    last = t.matmul(psA[:, po:po+512], WupS[:, kc*2048+c*128:kc*2048+(c+1)*128],
                                    latT[:, kc*512:(kc+1)*512], start=(kc == 0), stop=(kc == 1))
                return last
            P.op("pe", c_kv, reads=["WupS", "latT", bank], writes=[bank], tag=f"t{i}kv{c}")
            def c_kvev(a, c=c, po=po):
                dst = KVST[:, :].rearrange("p (g x) -> p g x", x=128)[:, :, c*8:(c+1)*8]
                return a.activation(dst, psA[:, po:po+512].rearrange("p (g t) -> p g t", t=8),
                                    AF.Identity, bias=BCOL[:, 10+c:11+c])
            P.op("act", c_kvev, reads=[bank, "BCOL", "consts", "KVST"], writes=["KVST"], tag=f"t{i}kvev{c}")

        def c_gate(t, xnTc=xnTc):
            last = None
            for p2, rhs in ((0, xnTc), (1, xnloT)):
                for kc in range(8):
                    last = t.matmul(psA[0:16, 2048:2560], GT[:, kc*16:(kc+1)*16],
                                    rhs[:, kc*512:(kc+1)*512],
                                    start=(p2 == 0 and kc == 0), stop=(p2 == 1 and kc == 7))
            return last
        P.op("pe", c_gate, reads=["GT", xkey, "xnloTh0", "xnloTh1", "b4"], writes=["b4"], tag=f"t{i}gate")
        P.op("dve", lambda v: v.tensor_copy(SCf[:, :], psA[0:16, 2048:2560]),
             reads=["b4", "SCf"], writes=["SCf"], tag=f"t{i}scf")
        def c_sct(t):
            last = None
            for s in range(4):
                last = t.transpose(psA[:, 2048+s*16:2048+(s+1)*16], SCf[:, s*128:(s+1)*128],
                                   identf[0:16, 0:16])
            return last
        P.op("pe", c_sct, reads=["SCf", "consts", "b4"], writes=["b4"], tag=f"t{i}sct")
        P.op("dve", lambda v: v.tensor_copy(scraw[:, :], psA[:, 2048:2112]),
             reads=["b4", "scraw"], writes=["scraw"], tag=f"t{i}scraw")
        def c_oha(v):
            last = None
            for s in range(4):
                last = v.tensor_tensor(sctmp[:, s*12:s*12+8], scraw[:, s*16:s*16+8],
                                       scraw[:, s*16+8:s*16+16], ALU.add)
            return last
        P.op("dve", c_oha, reads=["scraw", "sctmp"], writes=["sctmp"], tag=f"t{i}oha")
        def c_ohb(v):
            last = None
            for s in range(4):
                last = v.max(sctmp2[:, s*8:(s+1)*8], sctmp[:, s*12:s*12+8])
            return last
        P.op("dve", c_ohb, reads=["sctmp", "sctmp2"], writes=["sctmp2"], tag=f"t{i}ohb")
        def c_ohc(v):
            last = None
            for s in range(4):
                last = v.max_index(idxu[:, s*8:(s+1)*8], sctmp2[:, s*8:(s+1)*8], sctmp[:, s*12:s*12+8])
            return last
        P.op("dve", c_ohc, reads=["sctmp", "sctmp2", "idxu"], writes=["idxu"], tag=f"t{i}ohc")
        def c_ohd(v):
            last = None
            for s in range(4):
                last = v.tensor_copy(sctmp3[:, s:s+1], idxu[:, s*8:s*8+1])
            return last
        P.op("dve", c_ohd, reads=["idxu", "sctmp3"], writes=["sctmp3"], tag=f"t{i}ohd")
        P.op("pe", lambda t: t.transpose(psA[0:4, 2560:2688], sctmp3[:, 0:4], identf[:, :]),
             reads=["sctmp3", "consts", "b5"], writes=["b5"], tag=f"t{i}idxt")
        P.op("dve", lambda v: v.tensor_copy(SCf[0:4, 0:128], psA[0:4, 2560:2688]),
             reads=["b5", "idx4"], writes=["idx4"], tag=f"t{i}idxe")
        def c_ist(g, sem, par=par):
            g.dma_start(idx4_d[par].rearrange("(s c) -> s c", s=4), SCf[0:4, 0:128]).then_inc(sem, 16)
            return 1
        P.dma("pool", "mkd", c_ist, reads=["idx4"], writes=[f"idxd{par}"], tag=f"t{i}ist")

        def emit_head(q):
            r = q % 2
            base = q * 4
            spo = 1024
            sbank = "b2"
            vpo = 0
            def c_smm(t, base=base, spo=spo):
                last = None
                for gg in range(4):
                    gi = base + gg
                    last = t.matmul(psA[:, spo+gg*128:spo+gg*128+64],
                                    KVST[0:64, gi*128:(gi+1)*128],
                                    QEO[0:64, gi*64:(gi+1)*64], start=True, stop=True)
                    last = t.matmul(psA[:, spo+gg*128+64:spo+gg*128+128],
                                    KVST[0:64, gi*128:(gi+1)*128],
                                    XTR[0:64, gi*64:(gi+1)*64], start=True, stop=True)
                return last
            P.op("pe", c_smm, reads=["KVST", "QEO", "XTR", sbank], writes=[sbank], tag=f"t{i}smm{q}")
            P.op("act", (lambda a, r=r, spo=spo: a.activation(Xq[r][:, :], psA[:, spo:spo+512], AF.Exp,
                                                     bias=-CSTAB, scale=0.125)),
                 reads=[sbank, "consts", f"Xq{r}"], writes=[f"Xq{r}", sbank], tag=f"t{i}exp{q}")
            def c_v8t(t, base=base, vpo=vpo):
                last = None
                for gg in range(4):
                    gi = base + gg
                    last = t.transpose(psV[:, vpo+gg*64:vpo+(gg+1)*64],
                                       KVST[64:128, gi*128:(gi+1)*128], identb[64:128, 64:128])
                return last
            P.op("pe", c_v8t, reads=["KVST", "consts", "psBv0"], writes=["psBv0"], tag=f"t{i}v8t{q}")
            def c_v8e(v, r=r, vpo=vpo):
                return v.tensor_copy(V8a[r][:, :].rearrange("p (g c) -> p g c", c=65)[:, :, 0:64],
                                     psV[:, vpo:vpo+256].rearrange("p (g c) -> p g c", c=64))
            P.op("dve", c_v8e, reads=["psBv0", f"V8a{r}"], writes=[f"V8a{r}"], tag=f"t{i}v8e{q}")
        def emit_em(q):
            r = q % 2
            P.op("dve", (lambda v, r=r: v.tensor_tensor(
                    EMq[r][:, :], Xq[r][:, :], mask4[:, :], ALU.mult)),
                 reads=[f"Xq{r}", "mask4", f"EMq{r}"], writes=[f"EMq{r}"], tag=f"t{i}em{q}")
        def emit_tailA(q):
            r = q % 2
            ra = q % 3
            apo = 1536
            abank = "b3"
            def c_av(t, r=r, apo=apo):
                last = None
                for gg in range(4):
                    last = t.matmul(psA[:, apo+gg*65:apo+(gg+1)*65], EMq[r][:, gg*128:(gg+1)*128],
                                    V8a[r][:, gg*65:(gg+1)*65], start=True, stop=True)
                return last
            P.op("pe", c_av, reads=[f"EMq{r}", f"V8a{r}", abank], writes=[abank], tag=f"t{i}av{q}")
            P.op("dve", (lambda v, r=r, apo=apo: v.reciprocal(
                    rDv[r][:, :], psA[:, apo:apo+260].rearrange("p (g c) -> p g c", c=65)[:, :, 64:65])),
                 reads=[abank, f"rD{r}"], writes=[f"rD{r}"], tag=f"t{i}rd{q}")
            def c_avs(v, r=r, ra=ra, apo=apo):
                last = None
                for gg in range(4):
                    last = v.tensor_scalar(AVS[ra][:, gg*64:(gg+1)*64], psA[:, apo+gg*65:apo+gg*65+64],
                                           rDv[r][:, gg:gg+1], None, ALU.mult)
                return last
            P.op("dve", c_avs, reads=[abank, f"rD{r}", f"AVS{ra}"], writes=[f"AVS{ra}"], tag=f"t{i}avs{q}")
        def emit_avt(q):
            ra = q % 3
            def c_avt(t, ra=ra):
                last = None
                for gg in range(4):
                    last = t.transpose(psB[0:64, gg*64:(gg+1)*64],
                                       AVS[ra][0:64, gg*64:(gg+1)*64], identb[0:64, 0:64])
                    last = t.transpose(psB[64:128, gg*64:(gg+1)*64],
                                       AVS[ra][64:128, gg*64:(gg+1)*64], identb[64:128, 64:128])
                return last
            P.op("pe", c_avt, reads=[f"AVS{ra}", "consts", "psBa"], writes=["psBa"], tag=f"t{i}avt{q}")
        def emit_ost(q):
            base = q * 4
            def c_ost(a, base=base):
                last = None
                for gg in range(4):
                    gi = base + gg
                    srcE = psB[0:64, gg*64:(gg+1)*64].rearrange("p (c t) -> p c t", t=8)
                    dstE = OST[0:64, :].rearrange("p (c x) -> p c x", x=512)[:, :, gi*8:(gi+1)*8]
                    a.copy(dstE, srcE)
                    srcO = psB[64:128, gg*64:(gg+1)*64].rearrange("p (c t) -> p c t", t=8)
                    dstO = OST[64:128, :].rearrange("p (c x) -> p c x", x=512)[:, :, gi*8:(gi+1)*8]
                    last = a.copy(dstO, srcO)
                return last
            P.op("act", c_ost, reads=["psBa", "OST"], writes=["OST"], tag=f"t{i}ost{q}")
        for n in range(NQ + 3):
            if 0 <= n - 1 < NQ:
                emit_tailA(n - 1)
            if n < NQ:
                emit_head(n)
                emit_em(n)
            if 0 <= n - 3:
                emit_avt(n - 3)
                emit_ost(n - 3)

        def c_ild(g, sem, par=par):
            g.dma_start(IDXR[0:1, :], idx4_d[par].rearrange("(o f) -> o f", o=1)).then_inc(sem, 16)
            return 1
        P.dma("pool", "mkd", c_ild, reads=[f"idxd{par}", "IDXR"], writes=["IDXR"], tag=f"t{i}ild")
        P.op("pe", lambda t: t.matmul(psA[:, 2048:2560], ONES1[0:1, :], IDXR[0:1, :],
                                      start=True, stop=True),
             reads=["consts", "IDXR", "b4"], writes=["b4"], tag=f"t{i}idxb")
        for e in range(8):
            def c_mke(v, e=e):
                return v.tensor_scalar(MK[:, e*512:(e+1)*512], psA[:, 2048:2560], float(e),
                                       None, ALU.is_equal)
            P.op("dve", c_mke, reads=["b4", "MK"], writes=["MK"], tag=f"t{i}mke{e}")

        for mc in range(8):
            bank = ["b0", "b1"][mc % 2]
            po = [0, 512][mc % 2]
            def c_wo(t, mc=mc, po=po):
                last = None
                for kc in range(8):
                    last = t.matmul(psA[:, po:po+512], WoutS[:, kc*1024+mc*128:kc*1024+(mc+1)*128],
                                    OST[:, kc*512:(kc+1)*512], start=(kc == 0), stop=(kc == 7))
                return last
            P.op("pe", c_wo, reads=["WoutS", "OST", bank], writes=[bank], tag=f"t{i}wo{mc}")
            P.op("act", (lambda a, mc=mc, po=po: a.activation(
                    AACC[:, mc*512:(mc+1)*512], psA[:, po:po+512], AF.Identity,
                    bias=BCOL[:, 26+mc:27+mc])),
                 reads=[bank, "BCOL", "consts", "AACC"], writes=["AACC"], tag=f"t{i}woev{mc}")

        if i + 1 < n_tiles:
            P.dma("pool", "xin", mk_xin(i + 1), reads=["x"], writes=["x"], tag=f"xin{i+1}")

        def emit_p2_part(part, i=i, par=par, xnTc=xnTc, xkey=xkey):
            half_seq = [(tH, e, hf) for tH in range(2) for e in range(8) for hf in range(2)]
            def mk_wload(e, hf, r):
                def f(sy, sem):
                    src = WeT_d[e, hf*4:(hf+1)*4].rearrange("k p f -> p k f")
                    dst = slot[r][:, :].rearrange("p (k f) -> p k f", f=1024)
                    sy.dma_start(dst, src).then_inc(sem, 16)
                    return 1
                return f
            if part == 0:
                for j in range(3):
                    tH, e, hf = half_seq[j]
                    P.dma("sp", f"wexp{j % 3}", mk_wload(e, hf, j % 3), reads=[f"WeT{e}", f"slot{j%3}"],
                          writes=[f"slot{j%3}"], tag=f"t{i}wl{j}")
            tH = 0 if part < 2 else 1
            erange = range(0, 5) if part % 2 == 0 else range(5, 9)
            if True:
                mk = ("maskedA", "maskedB")
                mbufs = (maskedA, maskedB)
                def emit_prep(e, hfs, tH=tH):
                    rm = e % 2
                    mbuf = mbufs[rm]
                    mkey = mk[rm]
                    for hf in hfs:
                        def c_msk(v, e=e, hf=hf, tH=tH, mbuf=mbuf, xnTc=xnTc):
                            last = None
                            for k in range(4):
                                kc = hf*4 + k
                                last = v.tensor_tensor(
                                    mbuf[:, kc*256:(kc+1)*256],
                                    xnTc[:, kc*512+tH*256:kc*512+tH*256+256],
                                    MK[:, e*512+tH*256:e*512+tH*256+256], ALU.mult)
                            return last
                        P.op("dve", c_msk, reads=[xkey, "MK", f"{mkey}h{hf}"],
                             writes=[f"{mkey}h{hf}"], tag=f"t{i}msk{tH}_{e}_{hf}")
                def emit_emm(e, hfs, tH=tH):
                    rm = e % 2
                    mbuf = mbufs[rm]
                    mkey = mk[rm]
                    for hf in hfs:
                        j = tH*16 + e*2 + hf
                        for fh in range(2):
                            moe_keys = [f"moe{fc}" for fc in range(fh*4, fh*4+4)]
                            def c_emm(t, e=e, hf=hf, j=j, tH=tH, mbuf=mbuf, fh=fh):
                                last = None
                                for k in range(4):
                                    kc = hf*4 + k
                                    for fc in range(fh*4, fh*4+4):
                                        last = t.matmul(moe_ps(psA, fc),
                                                        slot[j % 3][:, k*1024+fc*128:k*1024+(fc+1)*128],
                                                        mbuf[:, kc*256:(kc+1)*256],
                                                        start=(e == 0 and kc == 0 and fc % 2 == 0),
                                                        stop=(e == 7 and kc == 7),
                                                        skip_group_check=True)
                                return last
                            bank_w = (["b0", "b1"] if fh == 0 else ["b2", "b3"]) \
                                if (tH == 0 and e == 0 and hf == 0) else []
                            P.op("pe", c_emm, reads=[f"slot{j%3}", f"{mkey}h{hf}"] + moe_keys + bank_w,
                                 writes=moe_keys + bank_w, tag=f"t{i}emm{tH}_{e}_{hf}_{fh}")
                        jn = j + 3
                        if jn < 32:
                            tHn, en, hfn = half_seq[jn]
                            P.dma("sp", f"wexp{jn % 3}", mk_wload(en, hfn, jn % 3),
                                  reads=[f"WeT{en}", f"slot{jn%3}"], writes=[f"slot{jn%3}"],
                                  tag=f"t{i}wl{jn}")
                for e in erange:
                    if e < 8:
                        emit_prep(e, (0,))
                    if e >= 1:
                        emit_emm(e - 1, (0,))
                    if e < 8:
                        emit_prep(e, (1,))
                    if e >= 1:
                        emit_emm(e - 1, (1,))
                if part % 2 == 0:
                    return
                def c_xrl(g, sem, i=i, tH=tH):
                    src = x_d[i*TT + tH*256:i*TT + (tH+1)*256, :].rearrange("(s p) f -> p s f", p=128)
                    dst = xr[:, :].rearrange("p (s f) -> p s f", f=1024)
                    g.dma_start(dst, src).then_inc(sem, 16)
                    return 1
                P.dma("pool", "xr", c_xrl, reads=["xrbuf"], writes=["xrbuf"], tag=f"t{i}xrl{tH}")
                for h in range(2):
                    s = tH*2 + h
                    def c_xab(v, h=h, s=s, par=par):
                        b0 = par*8 + s*2
                        return v.tensor_scalar(xr[:, h*1024:(h+1)*1024], xr[:, h*1024:(h+1)*1024],
                                               STT2[:, b0:b0+1], STT2[:, b0+1:b0+2],
                                               ALU.mult, ALU.add)
                    P.op("dve", c_xab, reads=["xrbuf", f"stt2{par}"], writes=["xrbuf"], tag=f"t{i}xab{tH}{h}")
                for fp in range(4):
                    fc = fp * 2
                    r = fp % 2
                    mkeys2 = [f"moe{fc}", f"moe{fc+1}"]
                    def c_fcomb(v, fc=fc, tH=tH, r=r):
                        asrc = AACC[:, :].rearrange("p (c x) -> p c x", x=512)[:, fc:fc+2,
                                                                              tH*256:(tH+1)*256]
                        return v.tensor_tensor(Fv[r][:, :].rearrange("p (c x) -> p c x", x=256),
                                               psA[:, fc*256:fc*256+512].rearrange(
                                                   "p (c x) -> p c x", x=256),
                                               asrc, ALU.add)
                    P.op("dve", c_fcomb, reads=mkeys2 + ["AACC", f"Fv{r}"],
                         writes=[f"Fv{r}"], tag=f"t{i}fc{tH}_{fp}")
                    def c_ft(t, fc=fc, r=r):
                        last = None
                        for q2 in range(4):
                            last = t.transpose(psA[:, fc*256+q2*128:fc*256+(q2+1)*128],
                                               Fv[r][:, q2*128:(q2+1)*128], identf[:, :])
                        return last
                    P.op("pe", c_ft, reads=[f"Fv{r}", "consts"] + mkeys2,
                         writes=mkeys2, tag=f"t{i}ft{tH}_{fp}")
                    for fc2 in (fc, fc + 1):
                        def c_fin(v, fc=fc2, tH=tH):
                            src = moe_ps(psA, fc).rearrange("p (h f) -> p h f", f=128)
                            xsv = xr[:, :].rearrange("p (s f) -> p s f", f=1024)[:, 0:2,
                                                                                fc*128:(fc+1)*128]
                            return v.tensor_tensor(xsv, src, xsv, ALU.add)
                        P.op("dve", c_fin, reads=[f"moe{fc2}", "xrbuf"],
                             writes=["xrbuf"], tag=f"t{i}fin{tH}_{fc2}")
                for h in range(2):
                    s = tH*2 + h
                    def c_st(g, sem, i=i, s=s, h=h):
                        g.dma_start(out_d[i*TT + s*128:i*TT + (s+1)*128, :],
                                    xr[:, h*1024:(h+1)*1024]).then_inc(sem, 16)
                        return 1
                    P.dma("pool", "xout", c_st, reads=["xrbuf"], writes=["outd"], tag=f"t{i}st{s}")
        for _pp in range(4):
            pend_p2.append(lambda p=_pp, f=emit_p2_part: f(p))


    while pend_p2:
        pend_p2.pop(0)()
    P.build(nc)
    nc._ctx_stack = ctx
    return nc, P

_CACHED = {}

def _get_nc():
    if "nc" not in _CACHED:
        nc, P = build_core(16)
        _CACHED["nc"] = nc
    return _CACHED["nc"]

def kernel(L0_attn_x, L0_attn_Wq, L0_attn_bq, L0_attn_Wdown, L0_attn_bdown,
           L0_attn_Wup, L0_attn_bup, L0_attn_Wout, L0_attn_bout,
           L0_ffn_gate_w, L0_ffn_expert_w):
    from concourse.bass_utils import run_bass_kernel_spmd
    x = np.ascontiguousarray(np.asarray(L0_attn_x, dtype=np.float32).reshape(65536, 1024))
    shared = {
        "Wq": np.ascontiguousarray(np.asarray(L0_attn_Wq, dtype=np.float32)),
        "bq": np.ascontiguousarray(np.asarray(L0_attn_bq, dtype=np.float32)),
        "Wdown": np.ascontiguousarray(np.asarray(L0_attn_Wdown, dtype=np.float32)),
        "bdown": np.ascontiguousarray(np.asarray(L0_attn_bdown, dtype=np.float32)),
        "Wup": np.ascontiguousarray(np.asarray(L0_attn_Wup, dtype=np.float32)),
        "bup": np.ascontiguousarray(np.asarray(L0_attn_bup, dtype=np.float32)),
        "Wout": np.ascontiguousarray(np.asarray(L0_attn_Wout, dtype=np.float32)),
        "bout": np.ascontiguousarray(np.asarray(L0_attn_bout, dtype=np.float32)),
        "gw": np.ascontiguousarray(np.asarray(L0_ffn_gate_w, dtype=np.float32)),
        "ew": np.ascontiguousarray(np.asarray(L0_ffn_expert_w, dtype=np.float32)),
    }
    in_maps = []
    for c in range(8):
        m = dict(shared)
        m["x"] = np.ascontiguousarray(x[c*8192:(c+1)*8192])
        in_maps.append(m)
    nc = _get_nc()
    res = run_bass_kernel_spmd(nc, in_maps, core_ids=list(range(8)))
    out = np.concatenate([res.results[c]["out"] for c in range(8)], axis=0)
    return out.reshape(16, 4096, 1024)


# version 19
# speedup vs baseline: 1.0741x; 1.0155x over previous
import contextlib

ENGINES = ("pe", "act", "dve", "pool", "sp")
ENG_ATTR = {"pe": "tensor", "act": "scalar", "dve": "vector", "pool": "gpsimd", "sp": "sync"}

class Op:
    __slots__ = ("idx", "engine", "fn", "reads", "writes", "is_dma", "ndma",
                 "stream", "deps", "need_inc", "inc_val", "tag")
    def __init__(self, idx, engine, fn, reads, writes, is_dma, ndma, stream, tag):
        self.idx = idx
        self.engine = engine
        self.fn = fn
        self.reads = tuple(reads)
        self.writes = tuple(writes)
        self.is_dma = is_dma
        self.ndma = ndma
        self.stream = stream
        self.deps = None
        self.need_inc = bool(is_dma)
        self.inc_val = None
        self.tag = tag

class Prog:
    def __init__(self):
        self.ops = []

    def op(self, engine, fn, reads=(), writes=(), tag=""):
        o = Op(len(self.ops), engine, fn, reads, writes, False, 0, None, tag)
        self.ops.append(o)
        return o

    def dma(self, engine, stream, fn, reads=(), writes=(), ndma=1, tag=""):
        o = Op(len(self.ops), engine, fn, reads, writes, True, ndma, stream, tag)
        self.ops.append(o)
        return o

    def _compute_deps(self):
        last_write = {}
        readers = {}
        for o in self.ops:
            deps = set()
            for r in o.reads:
                w = last_write.get(r)
                if w is not None:
                    deps.add(w)
            for w in o.writes:
                lw = last_write.get(w)
                if lw is not None:
                    deps.add(lw)
                for rd in readers.get(w, ()):
                    deps.add(rd)
            deps.discard(o.idx)
            o.deps = sorted(deps)
            for r in o.reads:
                readers.setdefault(r, []).append(o.idx)
            for w in o.writes:
                last_write[w] = o.idx
                readers[w] = []

    def _mark_incs(self):
        ops = self.ops
        for o in ops:
            for d in o.deps:
                dop = ops[d]
                if dop.is_dma:
                    continue
                if dop.engine != o.engine or dop.engine != "pe":
                    dop.need_inc = True
        counters = {}
        for o in ops:
            if o.need_inc:
                key = o.stream if o.is_dma else o.engine
                step = 16 * o.ndma if o.is_dma else 1
                counters[key] = counters.get(key, 0) + step
                o.inc_val = counters[key]
        self.totals = counters

    def build(self, nc, final_engine="pool"):
        self._compute_deps()
        self._mark_incs()
        ops = self.ops
        self._stack = contextlib.ExitStack()
        sems = {}
        keys = set()
        for o in ops:
            if o.need_inc:
                keys.add(o.stream if o.is_dma else o.engine)
        for k in sorted(keys):
            sems[k] = self._stack.enter_context(nc.semaphore(f"s_{k}"))
        self.sems = sems
        dma_streams = sorted({o.stream for o in ops if o.is_dma})
        block = self._stack.enter_context(nc.Block())

        def make_stream(ename):
            eops = [o for o in ops if o.engine == ename]
            def stream(eng):
                waited = {}
                stream_issued = {}
                for o in eops:
                    need = {}
                    for d in o.deps:
                        dop = ops[d]
                        if not dop.need_inc or dop.inc_val is None:
                            continue
                        if dop.engine == o.engine and not dop.is_dma and dop.engine == "pe":
                            continue
                        key = dop.stream if dop.is_dma else dop.engine
                        if dop.inc_val > need.get(key, -1):
                            need[key] = dop.inc_val
                    if o.is_dma:
                        prev = stream_issued.get(o.stream, 0)
                        if prev > need.get(o.stream, -1):
                            need[o.stream] = prev
                    for key, v in sorted(need.items()):
                        if waited.get(key, 0) >= v:
                            continue
                        eng.wait_ge(sems[key], v)
                        waited[key] = v
                    if o.is_dma:
                        n = o.fn(eng, sems[o.stream])
                        if n != o.ndma:
                            raise RuntimeError(
                                f"op {o.idx} {o.tag}: declared ndma={o.ndma} emitted {n}")
                        stream_issued[o.stream] = o.inc_val
                    else:
                        inst = o.fn(eng)
                        if o.need_inc:
                            if inst is None:
                                raise RuntimeError(
                                    f"op {o.idx} {o.tag}: fn returned None, inc needed")
                            inst.then_inc(sems[o.engine], 1)
                if ename == final_engine:
                    for s in dma_streams:
                        tot = self.totals.get(s, 0)
                        if tot and waited.get(s, 0) < tot:
                            eng.wait_ge(sems[s], tot)
            return stream

        used = {o.engine for o in ops}
        used.add(final_engine)
        for ename in ENGINES:
            if ename in used:
                getattr(block, ENG_ATTR[ename])(make_stream(ename))

    def stats(self):
        from collections import Counter
        c = Counter(o.engine for o in self.ops)
        ninc = sum(1 for o in self.ops if o.need_inc)
        return f"ops={len(self.ops)} by-engine={dict(c)} incs={ninc}"


"""Megakernel builder for mk_prb_d1 (MLA head-mix attention + argmax-MoE, depth 1).

Per-core program: ntok tokens (rows of x) in tiles of TT=512. Fully per-token
=> pure data-parallel across cores.

Math per token:
  xn = layernorm(x); q = xn@Wq+bq; lat = xn@Wdown+bdown; kv = lat@Wup+bup
  k,v per head from kv; RoPE dropped (same-position rotation cancels in q.k)
  att[h,g] = softmax_g(q_h.k_g/8); attn = (att@v)@Wout + bout
  scores = xn@gate^T (4-term bf16 hi/lo split, fp32 accum); e* = argmax (first)
  moe = xn@We*^T via diag-masked bf16 copies, PSUM-accumulated over experts
  out = x + xn + attn + moe     (ffn includes its own +xn residual)

Key layouts (per tile of 512 tokens = 4 slices of 128):
  xbuf/xsum token-major f32 [128 x 4x1024]
  xn slice lives in SQ[s%2] f32 [128 x 1024] (reused buffer)
  xnhi bf16 token-major [128 x 4096]
  xnT / xnloT feature-major bf16 [128 x 8c*512]  (chunk c at cols c*512)
  QEO [128 x 64g*64] bf16: group g cols g*64 + c*8 + t; rows 0:63 even heads
      (head 2c d), rows 64:127 odd heads  |  XTR [64 x ...] = QEO[64:128] dup at
      partition base 0 (DRAM bounce)
  KVST [128 x 64g*128]: col g*128 + head*8 + t; rows 0:63 k(d), 64:127 v(d)
  OST / AACC bf16 feature-major [128 x 8c*512] (chunk c = heads 2c,2c+1)
  masked half-tile [128 x 8kc*256] bf16 (one per expert, rotating 2; one is
      aliased into xnloT cols 0:2048 - gate is done before P2)
  WeT (We transposed, bf16) prebuilt in DRAM as [8e, 8kc, 128, 1024]

PSUM: psA f32 [128 x 3584] banks 0-6; psB bf16 [128 x 1024] bank 7.
  P1: b0/b1 (cols 0/512) proj rotate; b2 (1024) S-quad; b3 (1536) AVaug-quad;
      b4/b5 (2048/2560) xn-transpose quads; b6 (3072) bias-T/mask/gate/scoreT;
      psB: V8-quad cols 0:256, avT-quad cols 512:1024.
  P2 (per token-half tH of 256 tokens): moe[fc] = psA cols (fc//2)*512 +
      (fc%2)*256 (banks 0-3); diag rounds in psB ([128 x 256] x 4);
      F-transposes -> the just-freed moe region.
"""
import numpy as np
import concourse.bass as bass
import concourse.mybir as mybir

F32 = mybir.dt.float32
BF16 = mybir.dt.bfloat16
AF = mybir.ActivationFunctionType
ALU = mybir.AluOpType
AX = mybir.AxisListType

D = 1024
TT = 512
NG = TT // 8
NQ = NG // 4
CSTAB = 24.0
EPS = 1e-5

def moe_ps(psA, fc):
    return psA[:, (fc // 2) * 512 + (fc % 2) * 256:(fc // 2) * 512 + (fc % 2) * 256 + 256]

def build_core(n_tiles):
    ntok = n_tiles * TT
    nc = bass.Bass("TRN2", target_bir_lowering=False)

    x_d = nc.dram_tensor("x", [ntok, D], F32, kind="ExternalInput").ap()
    Wq_d = nc.dram_tensor("Wq", [D, D], F32, kind="ExternalInput").ap()
    bq_d = nc.dram_tensor("bq", [D], F32, kind="ExternalInput").ap()
    Wdown_d = nc.dram_tensor("Wdown", [D, 256], F32, kind="ExternalInput").ap()
    bdown_d = nc.dram_tensor("bdown", [256], F32, kind="ExternalInput").ap()
    Wup_d = nc.dram_tensor("Wup", [256, 2048], F32, kind="ExternalInput").ap()
    bup_d = nc.dram_tensor("bup", [2048], F32, kind="ExternalInput").ap()
    Wout_d = nc.dram_tensor("Wout", [D, D], F32, kind="ExternalInput").ap()
    bout_d = nc.dram_tensor("bout", [D], F32, kind="ExternalInput").ap()
    gw_d = nc.dram_tensor("gw", [8, D], F32, kind="ExternalInput").ap()
    ew_d = nc.dram_tensor("ew", [8, D, D], F32, kind="ExternalInput").ap()
    out_d = nc.dram_tensor("out", [ntok, D], F32, kind="ExternalOutput").ap()
    WeB_d = nc.dram_tensor("WeB", [8, D, D], BF16, kind="Internal").ap()
    WeT_d = nc.dram_tensor("WeT", [8, 8, 128, 1024], BF16, kind="Internal").ap()
    qdup_d = nc.dram_tensor("qdup", [2, 64, NG * 64], BF16, kind="Internal").ap()
    idx4_d = nc.dram_tensor("idx4", [2, 512], BF16, kind="Internal").ap()

    from contextlib import ExitStack
    ctx = ExitStack()
    E = ctx.enter_context
    WqS = E(nc.sbuf_tensor("WqS", [128, 8 * 1024], BF16))
    WdownS = E(nc.sbuf_tensor("WdownS", [128, 8 * 256], BF16))
    WupS = E(nc.sbuf_tensor("WupS", [128, 2 * 2048], BF16))
    WoutS = E(nc.sbuf_tensor("WoutS", [128, 8 * 1024], BF16))
    slot = [E(nc.sbuf_tensor(f"slot{r}", [128, 4 * 1024], BF16)) for r in range(3)]
    xbuf = E(nc.sbuf_tensor("xbuf", [128, 4096], F32))
    xr = E(nc.sbuf_tensor("xr", [128, 2048], F32))
    STT2 = E(nc.sbuf_tensor("STT2", [128, 16], F32))
    xnTb = [E(nc.sbuf_tensor(f"xnTb{r}", [128, 4096], BF16)) for r in range(2)]
    maskedA = E(nc.sbuf_tensor("maskedA", [128, 2048], BF16))
    maskedB = E(nc.sbuf_tensor("maskedB", [128, 2048], BF16))
    MK = E(nc.sbuf_tensor("MK", [128, 4096], BF16))
    IDXR = E(nc.sbuf_tensor("IDXR", [1, 512], BF16))
    ONES1 = E(nc.sbuf_tensor("ONES1", [1, 128], BF16))
    SQ_ = E(nc.sbuf_tensor("SQ", [128, 1024], F32)); SQ = [SQ_, SQ_]
    QEO = E(nc.sbuf_tensor("QEO", [128, NG * 64], BF16))
    XTR = E(nc.sbuf_tensor("XTR", [64, NG * 64], BF16))
    KVST = E(nc.sbuf_tensor("KVST", [128, NG * 128], BF16))
    latT = E(nc.sbuf_tensor("latT", [128, 2 * 512], BF16))
    OST = E(nc.sbuf_tensor("OST", [128, 8 * 512], BF16, is_transpose_dest=True))
    AACC = E(nc.sbuf_tensor("AACC", [128, 8 * 512], BF16, is_transpose_dest=True))
    xnloT = E(nc.sbuf_tensor("xnloT", [128, 4096], BF16))
    Xq = [E(nc.sbuf_tensor(f"Xq{r}", [128, 512], BF16)) for r in range(2)]
    V8a = [E(nc.sbuf_tensor(f"V8a{r}", [128, 4 * 65], BF16)) for r in range(2)]
    AVS = [E(nc.sbuf_tensor(f"AVS{r}", [128, 4 * 64], BF16)) for r in range(3)]
    rDv = [E(nc.sbuf_tensor(f"rD{r}", [128, 4], F32)) for r in range(2)]
    Fv = [E(nc.sbuf_tensor(f"Fv{r}", [128, 512], F32)) for r in range(2)]
    Rm = E(nc.sbuf_tensor("Rm", [8, 128], BF16))
    mask4 = E(nc.sbuf_tensor("mask4", [128, 512], BF16))
    EMq = [E(nc.sbuf_tensor(f"EMq{r}", [128, 512], BF16)) for r in range(2)]
    I8 = E(nc.sbuf_tensor("I8", [8, 8], BF16))
    identb = E(nc.sbuf_tensor("identb", [128, 128], BF16))
    identf = E(nc.sbuf_tensor("identf", [128, 128], F32))
    cpool = E(nc.sbuf_tensor("cpool", [128, 8], F32))
    onesf = E(nc.sbuf_tensor("onesf", [1, 1], F32))
    BR = E(nc.sbuf_tensor("BR", [34, 128], F32))
    BCOL = E(nc.sbuf_tensor("BCOL", [128, 34], F32))
    GT = E(nc.sbuf_tensor("GT", [128, 8 * 16], BF16))
    Grow = xr[0:8, 0:1024]
    Ghi = AACC[0:8, 0:1024]
    Glo = AACC[0:8, 1024:2048]
    SCf = E(nc.sbuf_tensor("SCf", [16, 512], F32))
    scraw = E(nc.sbuf_tensor("scraw", [128, 64], F32))
    sctmp = E(nc.sbuf_tensor("sctmp", [128, 48], F32))
    sctmp2 = E(nc.sbuf_tensor("sctmp2", [128, 32], F32))
    sctmp3 = E(nc.sbuf_tensor("sctmp3", [128, 4], F32))
    idxu = E(nc.sbuf_tensor("idxu", [128, 32], mybir.dt.uint32))
    STT = E(nc.sbuf_tensor("STT", [128, 32], F32))
    psA = E(nc.psum_tensor("psA", [128, 3072], F32))
    psV = E(nc.psum_tensor("psV", [128, 1024], BF16))
    psB = E(nc.psum_tensor("psB", [128, 1024], BF16))

    EXPBIAS = -24.0 - (14.125 * 14.125) / 8.0
    CONSTS = {0.125: 0, -CSTAB: 1, -0.5: 2, 1.0: 3, EXPBIAS: 4}
    for cv, idx in CONSTS.items():
        nc.const_aps.register(F32, cv, cpool[:, idx:idx+1])

    P = Prog()

    def c_mem1(g):
        for cv, idx in CONSTS.items():
            g.memset(cpool[:, idx:idx+1], cv)
        g.memset(onesf[:, :], 1.0)
        g.memset(ONES1[:, :], 1.0)
        g.memset(I8[:, :], 0.0)
        g.memset(identb[:, :], 0.0)
        g.memset(identf[:, :], 0.0)
        for r in range(2):
            g.memset(V8a[r][:, :].rearrange("p (g c) -> p g c", c=65)[:, :, 64:65], 1.0)
        return g.memset(IDXR[:, :], 0.0)
    P.op("pool", c_mem1, writes=["k0"], tag="mem1")
    def c_mem2(g):
        g.affine_select(out=I8[:, :], in_=I8[:, :], compare_op=ALU.not_equal,
                        fill=1.0, base=0, pattern=[[-1, 8]], channel_multiplier=1)
        g.affine_select(out=identb[:, :], in_=identb[:, :], compare_op=ALU.not_equal,
                        fill=1.0, base=0, pattern=[[-1, 128]], channel_multiplier=1)
        return g.affine_select(out=identf[:, :], in_=identf[:, :], compare_op=ALU.not_equal,
                               fill=1.0, base=0, pattern=[[-1, 128]], channel_multiplier=1)
    P.op("pool", c_mem2, reads=["k0"], writes=["consts"], tag="mem2")

    def c_srm(sy, sem):
        for j in range(16):
            sy.dma_start(Rm[:, 8*j:8*(j+1)], I8[:, :]).then_inc(sem, 16)
        sy.dma_start(Grow, gw_d).then_inc(sem, 16)
        sy.dma_start(BR[0:8, :], bq_d.rearrange("(a b) -> a b", b=128)).then_inc(sem, 16)
        sy.dma_start(BR[8:10, :], bdown_d.rearrange("(a b) -> a b", b=128)).then_inc(sem, 16)
        sy.dma_start(BR[10:26, :], bup_d.rearrange("(a b) -> a b", b=128)).then_inc(sem, 16)
        sy.dma_start(BR[26:34, :], bout_d.rearrange("(a b) -> a b", b=128)).then_inc(sem, 16)
        return 21
    P.dma("sp", "srm", c_srm, reads=["consts", "ostg"], writes=["Rm", "ostg", "BR", "xrbuf"], ndma=21, tag="srm")

    def c_wres(g, sem):
        for kc in range(8):
            g.dma_start(WqS[:, kc*1024:(kc+1)*1024], Wq_d[kc*128:(kc+1)*128, :]).then_inc(sem, 16)
        for kc in range(8):
            g.dma_start(WdownS[:, kc*256:(kc+1)*256], Wdown_d[kc*128:(kc+1)*128, :]).then_inc(sem, 16)
        for kc in range(2):
            g.dma_start(WupS[:, kc*2048:(kc+1)*2048], Wup_d[kc*128:(kc+1)*128, :]).then_inc(sem, 16)
        for kc in range(8):
            g.dma_start(WoutS[:, kc*1024:(kc+1)*1024], Wout_d[kc*128:(kc+1)*128, :]).then_inc(sem, 16)
        return 26
    P.dma("pool", "wcast", c_wres, writes=["WqS", "WdownS", "WupS", "WoutS"], ndma=26, tag="wres")
    def mk_xin(i):
        def f(g, sem):
            src = x_d[i*TT:(i+1)*TT, :].rearrange("(s p) f -> p s f", p=128)
            dst = xbuf[:, :].rearrange("p (s f) -> p s f", f=1024)
            g.dma_start(dst, src).then_inc(sem, 16)
            return 1
        return f
    P.dma("pool", "xin", mk_xin(0), writes=["x"], tag="xin0")

    P.op("dve", lambda v: v.tensor_copy(Ghi, Grow),
         reads=["ostg", "AACC"], writes=["AACC"], tag="ghi")
    P.op("dve", lambda v: v.tensor_tensor(Glo, Grow, Ghi, ALU.subtract),
         reads=["ostg", "xrbuf", "AACC"], writes=["AACC"], tag="glo")
    def c_gt(t):
        last = None
        for kc in range(8):
            last = t.transpose(psV[:, kc*16:kc*16+8], Ghi[:, kc*128:(kc+1)*128], identb[0:8, 0:8])
            last = t.transpose(psV[:, kc*16+8:kc*16+16], Glo[:, kc*128:(kc+1)*128], identb[0:8, 0:8])

        return last
    P.op("pe", c_gt, reads=["AACC", "consts", "psBv0"], writes=["psBv0"], tag="gtT")
    P.op("act", lambda s: s.copy(GT[:, :], psV[:, 0:128]),
         reads=["psBv0"], writes=["GT"], tag="gtev")

    for e in range(8):
        P.dma("pool", "wcast",
              (lambda g, sem, e=e: (g.dma_start(WeB_d[e], ew_d[e]).then_inc(sem, 16), 1)[1]),
              writes=[f"WeB{e}"], tag=f"webcast{e}")

    for e in range(8):
        strm = f"wprep{e % 2}"
        qoff = (e % 2) * 2048
        for rnd in range(4):
            def c_xp(sy, sem, e=e, rnd=rnd, qoff=qoff):
                for k in range(2):
                    c = rnd*2 + k
                    dst = AACC[:, qoff + k*1024: qoff + k*1024 + 1024]
                    sy.dma_start(dst, WeB_d[e][:, c*128:(c+1)*128], transpose=True).then_inc(sem, 16)
                return 2
            P.dma("sp", strm, c_xp, reads=[f"WeB{e}", f"wst{e%2}", "AACC"], writes=[f"wstw{e%2}", "AACC"],
                  ndma=2, tag=f"wxp{e}_{rnd}")
            def c_st(sy, sem, e=e, rnd=rnd, qoff=qoff):
                for k in range(2):
                    c = rnd*2 + k
                    src = AACC[:, qoff + k*1024: qoff + k*1024 + 1024]
                    sy.dma_start(WeT_d[e, c], src).then_inc(sem, 16)
                return 2
            P.dma("sp", strm, c_st, reads=[f"wstw{e%2}", "AACC"], writes=[f"wst{e%2}", f"WeT{e}", "AACC"],
                  ndma=2, tag=f"wst{e}_{rnd}")

    P.op("pe", lambda t: t.transpose(psA[:, 2048:2082], BR[0:34, :], identf[0:34, 0:34]),
         reads=["BR", "consts", "b4"], writes=["b4"], tag="biasmm")
    P.op("dve", lambda v: v.tensor_copy(BCOL[:, :], psA[:, 2048:2082]),
         reads=["b4"], writes=["BCOL"], tag="bcol")


    P.op("pe", lambda t: t.matmul(psA[:, 2048:2176], Rm[:, :], Rm[:, :], start=True, stop=True),
         reads=["Rm", "b4", "BCOL"], writes=["b4"], tag="maskmm")
    def c_mask4(v):
        last = None
        for j in range(4):
            last = v.tensor_copy(mask4[:, j*128:(j+1)*128], psA[:, 2048:2176])
        return last
    P.op("dve", c_mask4, reads=["b4"], writes=["mask4"], tag="mask4")


    pend_p2 = []
    for i in range(n_tiles):
        par = i % 2
        xnTc = xnTb[par]
        xkey = f"xnT{par}"
        if i > 0:
            pp = (i - 1) % 2
            pxnTc = xnTb[pp]
            pxkey = f"xnT{pp}"
            for e in range(2):
                pmbuf = (maskedA, maskedB)[e % 2]
                pmkey = ("maskedA", "maskedB")[e % 2]
                for hf in range(2):
                    def c_mskp(v, e=e, hf=hf, pmbuf=pmbuf, pxnTc=pxnTc):
                        last = None
                        for k in range(4):
                            kc = hf*4 + k
                            last = v.tensor_tensor(
                                pmbuf[:, kc*256:(kc+1)*256],
                                pxnTc[:, kc*512:kc*512+256],
                                MK[:, e*512:e*512+256], ALU.mult)
                        return last
                    P.op("dve", c_mskp, reads=[pxkey, "MK", f"{pmkey}h{hf}"],
                         writes=[f"{pmkey}h{hf}"], tag=f"t{i-1}msk0_{e}_{hf}")
        for s in range(4):
            if pend_p2:
                pend_p2.pop(0)()
            sr = s % 2
            xs = xbuf[:, s*1024:(s+1)*1024]
            st = STT[:, s*8:s*8+8]
            P.op("dve", (lambda v, xs=xs, st=st: v.reduce_sum(st[:, 0:1], xs, axis=AX.X)),
                 reads=["x", f"st{s}"], writes=[f"st{s}"], tag=f"t{i}sum{s}")
            P.op("act", (lambda a, xs=xs, sr=sr: a.activation(SQ[sr][:, :], xs, AF.Square)),
                 reads=["x", "SQk"], writes=["SQk"], tag=f"t{i}sq{s}")
            P.op("dve", (lambda v, st=st, sr=sr: v.reduce_sum(st[:, 1:2], SQ[sr][:, :], axis=AX.X)),
                 reads=["SQk", f"st{s}"], writes=[f"st{s}"], tag=f"t{i}sqs{s}")
            P.op("dve", (lambda v, st=st: v.tensor_scalar(st[:, 2:3], st[:, 0:1], 1.0 / D, None, ALU.mult)),
                 reads=[f"st{s}"], writes=[f"st{s}"], tag=f"t{i}mean{s}")
            P.op("dve", (lambda v, st=st: v.tensor_scalar(st[:, 5:6], st[:, 2:3], st[:, 2:3], EPS,
                                                          ALU.mult, ALU.subtract)),
                 reads=[f"st{s}"], writes=[f"st{s}"], tag=f"t{i}msq{s}")
            P.op("dve", (lambda v, st=st: v.scalar_tensor_tensor(st[:, 6:7], st[:, 1:2], 1.0 / D,
                                                                 st[:, 5:6], ALU.mult, ALU.subtract)),
                 reads=[f"st{s}"], writes=[f"st{s}"], tag=f"t{i}var{s}")
            P.op("act", (lambda a, st=st: a.activation(st[:, 7:8], st[:, 6:7], AF.Ln)),
                 reads=[f"st{s}", "consts"], writes=[f"st{s}"], tag=f"t{i}ln{s}")
            P.op("act", (lambda a, st=st: a.activation(st[:, 7:8], st[:, 7:8], AF.Exp, scale=-0.5)),
                 reads=[f"st{s}", "consts"], writes=[f"st{s}"], tag=f"t{i}rstd{s}")
            def c_norm(v, xs=xs, st=st, sr=sr):
                return v.tensor_scalar(SQ[sr][:, :], xs, st[:, 2:3], st[:, 7:8],
                                       ALU.subtract, ALU.mult)
            P.op("dve", c_norm, reads=["x", f"st{s}", "SQk"], writes=["SQk"], tag=f"t{i}norm{s}")
            def c_ab(v, st=st, s=s, par=par):
                b0 = par*8 + s*2
                v.tensor_scalar(STT2[:, b0:b0+1], st[:, 7:8], 1.0, None, ALU.add)
                return v.tensor_scalar(STT2[:, b0+1:b0+2], st[:, 2:3], st[:, 7:8], -1.0,
                                       ALU.mult, ALU.mult)
            P.op("dve", c_ab, reads=[f"st{s}", f"stt2{par}"], writes=[f"stt2{par}"], tag=f"t{i}ab{s}")
            def c_xnt(t, sr=sr):
                last = None
                for c in range(8):
                    po = 2048 + (c // 4) * 512 + (c % 4) * 128
                    last = t.transpose(psA[:, po:po+128], SQ[sr][:, c*128:(c+1)*128], identf[:, :])
                return last
            P.op("pe", c_xnt, reads=["SQk", "consts", "b4", "b5"],
                 writes=["b4", "b5"], tag=f"t{i}xnt{s}")
            for half in range(2):
                po = 2048 + half * 512
                bk = "b4" if half == 0 else "b5"
                def c_hev(a, s=s, half=half, po=po, xnTc=xnTc):
                    dst = xnTc[:, :].rearrange("p (c x) -> p c x", x=512)[:, half*4:(half+1)*4,
                                                                        s*128:(s+1)*128]
                    return a.copy(dst, psA[:, po:po+512].rearrange("p (c x) -> p c x", x=128))
                P.op("act", c_hev, reads=[bk, xkey], writes=[xkey], tag=f"t{i}hev{s}{half}")
                def c_lev(v, s=s, half=half, po=po, xnTc=xnTc):
                    dst = xnloT[:, :].rearrange("p (c x) -> p c x", x=512)[:, half*4:(half+1)*4,
                                                                          s*128:(s+1)*128]
                    srcT = xnTc[:, :].rearrange("p (c x) -> p c x", x=512)[:, half*4:(half+1)*4,
                                                                         s*128:(s+1)*128]
                    return v.tensor_tensor(dst, psA[:, po:po+512].rearrange("p (c x) -> p c x", x=128),
                                           srcT, ALU.subtract)
                P.op("dve", c_lev, reads=[bk, xkey, "xnloTh0", "xnloTh1"],
                     writes=["xnloTh0", "xnloTh1"], tag=f"t{i}lev{s}{half}")
        while pend_p2:
            pend_p2.pop(0)()



        for mc in range(8):
            bank = ["b0", "b1"][mc % 2]
            po = [0, 512][mc % 2]
            def c_wq(t, mc=mc, po=po, xnTc=xnTc):
                last = None
                for kc in range(8):
                    last = t.matmul(psA[:, po:po+512], WqS[:, kc*1024+mc*128:kc*1024+(mc+1)*128],
                                    xnTc[:, kc*512:(kc+1)*512], start=(kc == 0), stop=(kc == 7))
                return last
            P.op("pe", c_wq, reads=["WqS", xkey, bank], writes=[bank], tag=f"t{i}wq{mc}")
            def c_qev(v, mc=mc, po=po):
                dst = QEO[:, :].rearrange("p (g x) -> p g x", x=64)[:, :, mc*8:(mc+1)*8]
                return v.tensor_scalar(dst, psA[:, po:po+512].rearrange("p (g t) -> p g t", t=8),
                                       BCOL[:, mc:mc+1], None, ALU.add)
            P.op("dve", c_qev, reads=[bank, "BCOL", "QEO"], writes=["QEO"], tag=f"t{i}qev{mc}")
        P.dma("pool", "qd",
              (lambda g, sem, par=par: (g.dma_start(qdup_d[par], QEO[64:128, :]).then_inc(sem, 16), 1)[1]),
              reads=["QEO"], writes=[f"qdram{par}"], tag=f"t{i}qdst")
        P.dma("pool", "qd",
              (lambda g, sem, par=par: (g.dma_start(XTR[:, :], qdup_d[par]).then_inc(sem, 16), 1)[1]),
              reads=[f"qdram{par}", "XTR"], writes=["XTR"], tag=f"t{i}qdld")

        for mc in range(2):
            bank = ["b0", "b1"][mc]
            po = [0, 512][mc]
            def c_wd(t, mc=mc, po=po, xnTc=xnTc):
                last = None
                for kc in range(8):
                    last = t.matmul(psA[:, po:po+512], WdownS[:, kc*256+mc*128:kc*256+(mc+1)*128],
                                    xnTc[:, kc*512:(kc+1)*512], start=(kc == 0), stop=(kc == 7))
                return last
            P.op("pe", c_wd, reads=["WdownS", xkey, bank], writes=[bank], tag=f"t{i}wd{mc}")
            P.op("dve", (lambda v, mc=mc, po=po: v.tensor_scalar(
                    latT[:, mc*512:(mc+1)*512], psA[:, po:po+512], BCOL[:, 8+mc:9+mc], None, ALU.add)),
                 reads=[bank, "BCOL", "latT"], writes=["latT"], tag=f"t{i}dev{mc}")

        for c in range(16):
            bank = ["b0", "b1"][c % 2]
            po = [0, 512][c % 2]
            def c_kv(t, c=c, po=po):
                last = None
                for kc in range(2):
                    last = t.matmul(psA[:, po:po+512], WupS[:, kc*2048+c*128:kc*2048+(c+1)*128],
                                    latT[:, kc*512:(kc+1)*512], start=(kc == 0), stop=(kc == 1))
                return last
            P.op("pe", c_kv, reads=["WupS", "latT", bank], writes=[bank], tag=f"t{i}kv{c}")
            def c_kvev(a, c=c, po=po):
                dst = KVST[:, :].rearrange("p (g x) -> p g x", x=128)[:, :, c*8:(c+1)*8]
                return a.activation(dst, psA[:, po:po+512].rearrange("p (g t) -> p g t", t=8),
                                    AF.Identity, bias=BCOL[:, 10+c:11+c])
            P.op("act", c_kvev, reads=[bank, "BCOL", "consts", "KVST"], writes=["KVST"], tag=f"t{i}kvev{c}")

        def c_gate(t, xnTc=xnTc):
            last = None
            for p2, rhs in ((0, xnTc), (1, xnloT)):
                for kc in range(8):
                    last = t.matmul(psA[0:16, 2048:2560], GT[:, kc*16:(kc+1)*16],
                                    rhs[:, kc*512:(kc+1)*512],
                                    start=(p2 == 0 and kc == 0), stop=(p2 == 1 and kc == 7))
            return last
        P.op("pe", c_gate, reads=["GT", xkey, "xnloTh0", "xnloTh1", "b4"], writes=["b4"], tag=f"t{i}gate")
        P.op("dve", lambda v: v.tensor_copy(SCf[:, :], psA[0:16, 2048:2560]),
             reads=["b4", "SCf"], writes=["SCf"], tag=f"t{i}scf")
        def c_sct(t):
            last = None
            for s in range(4):
                last = t.transpose(psA[:, 2048+s*16:2048+(s+1)*16], SCf[:, s*128:(s+1)*128],
                                   identf[0:16, 0:16])
            return last
        P.op("pe", c_sct, reads=["SCf", "consts", "b4"], writes=["b4"], tag=f"t{i}sct")
        P.op("dve", lambda v: v.tensor_copy(scraw[:, :], psA[:, 2048:2112]),
             reads=["b4", "scraw"], writes=["scraw"], tag=f"t{i}scraw")
        def c_oha(v):
            last = None
            for s in range(4):
                last = v.tensor_tensor(sctmp[:, s*12:s*12+8], scraw[:, s*16:s*16+8],
                                       scraw[:, s*16+8:s*16+16], ALU.add)
            return last
        P.op("dve", c_oha, reads=["scraw", "sctmp"], writes=["sctmp"], tag=f"t{i}oha")
        def c_ohb(v):
            last = None
            for s in range(4):
                last = v.max(sctmp2[:, s*8:(s+1)*8], sctmp[:, s*12:s*12+8])
            return last
        P.op("dve", c_ohb, reads=["sctmp", "sctmp2"], writes=["sctmp2"], tag=f"t{i}ohb")
        def c_ohc(v):
            last = None
            for s in range(4):
                last = v.max_index(idxu[:, s*8:(s+1)*8], sctmp2[:, s*8:(s+1)*8], sctmp[:, s*12:s*12+8])
            return last
        P.op("dve", c_ohc, reads=["sctmp", "sctmp2", "idxu"], writes=["idxu"], tag=f"t{i}ohc")
        def c_ohd(v):
            last = None
            for s in range(4):
                last = v.tensor_copy(sctmp3[:, s:s+1], idxu[:, s*8:s*8+1])
            return last
        P.op("dve", c_ohd, reads=["idxu", "sctmp3"], writes=["sctmp3"], tag=f"t{i}ohd")
        P.op("pe", lambda t: t.transpose(psA[0:4, 2560:2688], sctmp3[:, 0:4], identf[:, :]),
             reads=["sctmp3", "consts", "b5"], writes=["b5"], tag=f"t{i}idxt")
        P.op("dve", lambda v: v.tensor_copy(SCf[0:4, 0:128], psA[0:4, 2560:2688]),
             reads=["b5", "idx4"], writes=["idx4"], tag=f"t{i}idxe")
        def c_ist(g, sem, par=par):
            g.dma_start(idx4_d[par].rearrange("(s c) -> s c", s=4), SCf[0:4, 0:128]).then_inc(sem, 16)
            return 1
        P.dma("pool", "mkd", c_ist, reads=["idx4"], writes=[f"idxd{par}"], tag=f"t{i}ist")

        def emit_head(q):
            r = q % 2
            base = q * 4
            spo = 1024
            sbank = "b2"
            vpo = 0
            def c_smm(t, base=base, spo=spo):
                last = None
                for gg in range(4):
                    gi = base + gg
                    last = t.matmul(psA[:, spo+gg*128:spo+gg*128+64],
                                    KVST[0:64, gi*128:(gi+1)*128],
                                    QEO[0:64, gi*64:(gi+1)*64], start=True, stop=True)
                    last = t.matmul(psA[:, spo+gg*128+64:spo+gg*128+128],
                                    KVST[0:64, gi*128:(gi+1)*128],
                                    XTR[0:64, gi*64:(gi+1)*64], start=True, stop=True)
                return last
            P.op("pe", c_smm, reads=["KVST", "QEO", "XTR", sbank], writes=[sbank], tag=f"t{i}smm{q}")
            P.op("act", (lambda a, r=r, spo=spo: a.activation(Xq[r][:, :], psA[:, spo:spo+512], AF.Exp,
                                                     bias=-CSTAB, scale=0.125)),
                 reads=[sbank, "consts", f"Xq{r}"], writes=[f"Xq{r}", sbank], tag=f"t{i}exp{q}")
            def c_v8t(t, base=base, vpo=vpo):
                last = None
                for gg in range(4):
                    gi = base + gg
                    last = t.transpose(psV[:, vpo+gg*64:vpo+(gg+1)*64],
                                       KVST[64:128, gi*128:(gi+1)*128], identb[64:128, 64:128])
                return last
            P.op("pe", c_v8t, reads=["KVST", "consts", "psBv0"], writes=["psBv0"], tag=f"t{i}v8t{q}")
            def c_v8e(v, r=r, vpo=vpo):
                return v.tensor_copy(V8a[r][:, :].rearrange("p (g c) -> p g c", c=65)[:, :, 0:64],
                                     psV[:, vpo:vpo+256].rearrange("p (g c) -> p g c", c=64))
            P.op("dve", c_v8e, reads=["psBv0", f"V8a{r}"], writes=[f"V8a{r}"], tag=f"t{i}v8e{q}")
        def emit_em(q):
            r = q % 2
            P.op("dve", (lambda v, r=r: v.tensor_tensor(
                    EMq[r][:, :], Xq[r][:, :], mask4[:, :], ALU.mult)),
                 reads=[f"Xq{r}", "mask4", f"EMq{r}"], writes=[f"EMq{r}"], tag=f"t{i}em{q}")
        def emit_tailA(q):
            r = q % 2
            ra = q % 3
            apo = 1536
            abank = "b3"
            def c_av(t, r=r, apo=apo):
                last = None
                for gg in range(4):
                    last = t.matmul(psA[:, apo+gg*65:apo+(gg+1)*65], EMq[r][:, gg*128:(gg+1)*128],
                                    V8a[r][:, gg*65:(gg+1)*65], start=True, stop=True)
                return last
            P.op("pe", c_av, reads=[f"EMq{r}", f"V8a{r}", abank], writes=[abank], tag=f"t{i}av{q}")
            P.op("dve", (lambda v, r=r, apo=apo: v.reciprocal(
                    rDv[r][:, :], psA[:, apo:apo+260].rearrange("p (g c) -> p g c", c=65)[:, :, 64:65])),
                 reads=[abank, f"rD{r}"], writes=[f"rD{r}"], tag=f"t{i}rd{q}")
            def c_avs(v, r=r, ra=ra, apo=apo):
                last = None
                for gg in range(4):
                    last = v.tensor_scalar(AVS[ra][:, gg*64:(gg+1)*64], psA[:, apo+gg*65:apo+gg*65+64],
                                           rDv[r][:, gg:gg+1], None, ALU.mult)
                return last
            P.op("dve", c_avs, reads=[abank, f"rD{r}", f"AVS{ra}"], writes=[f"AVS{ra}"], tag=f"t{i}avs{q}")
        def emit_avt(q):
            ra = q % 3
            def c_avt(t, ra=ra):
                last = None
                for gg in range(4):
                    last = t.transpose(psB[0:64, gg*64:(gg+1)*64],
                                       AVS[ra][0:64, gg*64:(gg+1)*64], identb[0:64, 0:64])
                    last = t.transpose(psB[64:128, gg*64:(gg+1)*64],
                                       AVS[ra][64:128, gg*64:(gg+1)*64], identb[64:128, 64:128])
                return last
            P.op("pe", c_avt, reads=[f"AVS{ra}", "consts", "psBa"], writes=["psBa"], tag=f"t{i}avt{q}")
        def emit_ost(q):
            base = q * 4
            def src_of(gg, par2):
                return psB[par2*64:par2*64+64, gg*64:(gg+1)*64].rearrange("p (c t) -> p c t", t=8)
            def dst_of(gg, par2, base=base):
                gi = base + gg
                return OST[par2*64:par2*64+64, :].rearrange(
                    "p (c x) -> p c x", x=512)[:, :, gi*8:(gi+1)*8]
            acts = (((0, 0), (0, 1), (1, 0), (1, 1))
                    if q >= NQ - 6 else
                    ((0, 0), (0, 1), (1, 0), (1, 1), (2, 0), (2, 1), (3, 0), (3, 1)))
            dves = (((2, 0), (2, 1), (3, 0), (3, 1)) if q >= NQ - 6 else ())
            def c_ost(a, acts=acts):
                last = None
                for gg, par2 in acts:
                    last = a.copy(dst_of(gg, par2), src_of(gg, par2))
                return last
            P.op("act", c_ost, reads=["psBa", "OST"], writes=["OST"], tag=f"t{i}ost{q}")
            for nb, (gg, par2) in enumerate(dves):
                def c_osb(v, gg=gg, par2=par2):
                    return v.tensor_copy(dst_of(gg, par2), src_of(gg, par2))
                P.op("dve", c_osb, reads=["psBa", "OST"], writes=["OST"], tag=f"t{i}osb{q}_{nb}")
        for n in range(NQ + 3):
            if 0 <= n - 1 < NQ:
                emit_tailA(n - 1)
            if n < NQ:
                emit_head(n)
                emit_em(n)
            if 0 <= n - 3:
                emit_avt(n - 3)
                emit_ost(n - 3)

        def c_ild(g, sem, par=par):
            g.dma_start(IDXR[0:1, :], idx4_d[par].rearrange("(o f) -> o f", o=1)).then_inc(sem, 16)
            return 1
        P.dma("pool", "mkd", c_ild, reads=[f"idxd{par}", "IDXR"], writes=["IDXR"], tag=f"t{i}ild")
        P.op("pe", lambda t: t.matmul(psA[:, 2048:2560], ONES1[0:1, :], IDXR[0:1, :],
                                      start=True, stop=True),
             reads=["consts", "IDXR", "b4"], writes=["b4"], tag=f"t{i}idxb")
        for e in range(8):
            def c_mke(v, e=e):
                return v.tensor_scalar(MK[:, e*512:(e+1)*512], psA[:, 2048:2560], float(e),
                                       None, ALU.is_equal)
            P.op("dve", c_mke, reads=["b4", "MK"], writes=["MK"], tag=f"t{i}mke{e}")

        for mc in range(8):
            bank = ["b0", "b1"][mc % 2]
            po = [0, 512][mc % 2]
            def c_wo(t, mc=mc, po=po):
                last = None
                for kc in range(8):
                    last = t.matmul(psA[:, po:po+512], WoutS[:, kc*1024+mc*128:kc*1024+(mc+1)*128],
                                    OST[:, kc*512:(kc+1)*512], start=(kc == 0), stop=(kc == 7))
                return last
            P.op("pe", c_wo, reads=["WoutS", "OST", bank], writes=[bank], tag=f"t{i}wo{mc}")
            P.op("act", (lambda a, mc=mc, po=po: a.activation(
                    AACC[:, mc*512:(mc+1)*512], psA[:, po:po+512], AF.Identity,
                    bias=BCOL[:, 26+mc:27+mc])),
                 reads=[bank, "BCOL", "consts", "AACC"], writes=["AACC"], tag=f"t{i}woev{mc}")

        if i + 1 < n_tiles:
            P.dma("pool", "xin", mk_xin(i + 1), reads=["x"], writes=["x"], tag=f"xin{i+1}")

        def emit_p2_part(part, i=i, par=par, xnTc=xnTc, xkey=xkey):
            half_seq = [(tH, e, hf) for tH in range(2) for e in range(8) for hf in range(2)]
            def mk_wload(e, hf, r):
                def f(sy, sem):
                    src = WeT_d[e, hf*4:(hf+1)*4].rearrange("k p f -> p k f")
                    dst = slot[r][:, :].rearrange("p (k f) -> p k f", f=1024)
                    sy.dma_start(dst, src).then_inc(sem, 16)
                    return 1
                return f
            if part == 0:
                for j in range(3):
                    tH, e, hf = half_seq[j]
                    P.dma("sp", f"wexp{j % 3}", mk_wload(e, hf, j % 3), reads=[f"WeT{e}", f"slot{j%3}"],
                          writes=[f"slot{j%3}"], tag=f"t{i}wl{j}")
            tH = 0 if part < 2 else 1
            erange = range(0, 5) if part % 2 == 0 else range(5, 9)
            if True:
                mk = ("maskedA", "maskedB")
                mbufs = (maskedA, maskedB)
                def emit_prep(e, hfs, tH=tH):
                    rm = e % 2
                    mbuf = mbufs[rm]
                    mkey = mk[rm]
                    for hf in hfs:
                        def c_msk(v, e=e, hf=hf, tH=tH, mbuf=mbuf, xnTc=xnTc):
                            last = None
                            for k in range(4):
                                kc = hf*4 + k
                                last = v.tensor_tensor(
                                    mbuf[:, kc*256:(kc+1)*256],
                                    xnTc[:, kc*512+tH*256:kc*512+tH*256+256],
                                    MK[:, e*512+tH*256:e*512+tH*256+256], ALU.mult)
                            return last
                        P.op("dve", c_msk, reads=[xkey, "MK", f"{mkey}h{hf}"],
                             writes=[f"{mkey}h{hf}"], tag=f"t{i}msk{tH}_{e}_{hf}")
                def emit_emm(e, hfs, tH=tH):
                    rm = e % 2
                    mbuf = mbufs[rm]
                    mkey = mk[rm]
                    for hf in hfs:
                        j = tH*16 + e*2 + hf
                        for fh in range(2):
                            moe_keys = [f"moe{fc}" for fc in range(fh*4, fh*4+4)]
                            def c_emm(t, e=e, hf=hf, j=j, tH=tH, mbuf=mbuf, fh=fh):
                                last = None
                                for k in range(4):
                                    kc = hf*4 + k
                                    for fc in range(fh*4, fh*4+4):
                                        last = t.matmul(moe_ps(psA, fc),
                                                        slot[j % 3][:, k*1024+fc*128:k*1024+(fc+1)*128],
                                                        mbuf[:, kc*256:(kc+1)*256],
                                                        start=(e == 0 and kc == 0 and fc % 2 == 0),
                                                        stop=(e == 7 and kc == 7),
                                                        skip_group_check=True)
                                return last
                            bank_w = (["b0", "b1"] if fh == 0 else ["b2", "b3"]) \
                                if (tH == 0 and e == 0 and hf == 0) else []
                            P.op("pe", c_emm, reads=[f"slot{j%3}", f"{mkey}h{hf}"] + moe_keys + bank_w,
                                 writes=moe_keys + bank_w, tag=f"t{i}emm{tH}_{e}_{hf}_{fh}")
                        jn = j + 3
                        if jn < 32:
                            tHn, en, hfn = half_seq[jn]
                            P.dma("sp", f"wexp{jn % 3}", mk_wload(en, hfn, jn % 3),
                                  reads=[f"WeT{en}", f"slot{jn%3}"], writes=[f"slot{jn%3}"],
                                  tag=f"t{i}wl{jn}")
                for e in erange:
                    skip_pre = (part == 0 and e < 2)
                    if e < 8 and not skip_pre:
                        emit_prep(e, (0,))
                    if e >= 1:
                        emit_emm(e - 1, (0,))
                    if e < 8 and not skip_pre:
                        emit_prep(e, (1,))
                    if e >= 1:
                        emit_emm(e - 1, (1,))
                if part % 2 == 0:
                    return
                def c_xrl(g, sem, i=i, tH=tH):
                    src = x_d[i*TT + tH*256:i*TT + (tH+1)*256, :].rearrange("(s p) f -> p s f", p=128)
                    dst = xr[:, :].rearrange("p (s f) -> p s f", f=1024)
                    g.dma_start(dst, src).then_inc(sem, 16)
                    return 1
                P.dma("pool", "xr", c_xrl, reads=["xrbuf"], writes=["xrbuf"], tag=f"t{i}xrl{tH}")
                for h in range(2):
                    s = tH*2 + h
                    def c_xab(v, h=h, s=s, par=par):
                        b0 = par*8 + s*2
                        return v.tensor_scalar(xr[:, h*1024:(h+1)*1024], xr[:, h*1024:(h+1)*1024],
                                               STT2[:, b0:b0+1], STT2[:, b0+1:b0+2],
                                               ALU.mult, ALU.add)
                    P.op("dve", c_xab, reads=["xrbuf", f"stt2{par}"], writes=["xrbuf"], tag=f"t{i}xab{tH}{h}")
                for fp in range(4):
                    fc = fp * 2
                    r = fp % 2
                    mkeys2 = [f"moe{fc}", f"moe{fc+1}"]
                    def c_fcomb(v, fc=fc, tH=tH, r=r):
                        asrc = AACC[:, :].rearrange("p (c x) -> p c x", x=512)[:, fc:fc+2,
                                                                              tH*256:(tH+1)*256]
                        return v.tensor_tensor(Fv[r][:, :].rearrange("p (c x) -> p c x", x=256),
                                               psA[:, fc*256:fc*256+512].rearrange(
                                                   "p (c x) -> p c x", x=256),
                                               asrc, ALU.add)
                    P.op("dve", c_fcomb, reads=mkeys2 + ["AACC", f"Fv{r}"],
                         writes=[f"Fv{r}"], tag=f"t{i}fc{tH}_{fp}")
                    def c_ft(t, fc=fc, r=r):
                        last = None
                        for q2 in range(4):
                            last = t.transpose(psA[:, fc*256+q2*128:fc*256+(q2+1)*128],
                                               Fv[r][:, q2*128:(q2+1)*128], identf[:, :])
                        return last
                    P.op("pe", c_ft, reads=[f"Fv{r}", "consts"] + mkeys2,
                         writes=mkeys2, tag=f"t{i}ft{tH}_{fp}")
                    for fc2 in (fc, fc + 1):
                        def c_fin(v, fc=fc2, tH=tH):
                            src = moe_ps(psA, fc).rearrange("p (h f) -> p h f", f=128)
                            xsv = xr[:, :].rearrange("p (s f) -> p s f", f=1024)[:, 0:2,
                                                                                fc*128:(fc+1)*128]
                            return v.tensor_tensor(xsv, src, xsv, ALU.add)
                        P.op("dve", c_fin, reads=[f"moe{fc2}", "xrbuf"],
                             writes=["xrbuf"], tag=f"t{i}fin{tH}_{fc2}")
                for h in range(2):
                    s = tH*2 + h
                    def c_st(g, sem, i=i, s=s, h=h):
                        g.dma_start(out_d[i*TT + s*128:i*TT + (s+1)*128, :],
                                    xr[:, h*1024:(h+1)*1024]).then_inc(sem, 16)
                        return 1
                    P.dma("pool", "xout", c_st, reads=["xrbuf"], writes=["outd"], tag=f"t{i}st{s}")
        for _pp in range(4):
            pend_p2.append(lambda p=_pp, f=emit_p2_part: f(p))


    pp = (n_tiles - 1) % 2
    pxnTc = xnTb[pp]
    pxkey = f"xnT{pp}"
    for e in range(2):
        pmbuf = (maskedA, maskedB)[e % 2]
        pmkey = ("maskedA", "maskedB")[e % 2]
        for hf in range(2):
            def c_mskp(v, e=e, hf=hf, pmbuf=pmbuf, pxnTc=pxnTc):
                last = None
                for k in range(4):
                    kc = hf*4 + k
                    last = v.tensor_tensor(
                        pmbuf[:, kc*256:(kc+1)*256],
                        pxnTc[:, kc*512:kc*512+256],
                        MK[:, e*512:e*512+256], ALU.mult)
                return last
            P.op("dve", c_mskp, reads=[pxkey, "MK", f"{pmkey}h{hf}"],
                 writes=[f"{pmkey}h{hf}"], tag=f"t{n_tiles-1}msk0_{e}_{hf}")
    while pend_p2:
        pend_p2.pop(0)()
    P.build(nc)
    nc._ctx_stack = ctx
    return nc, P

_CACHED = {}

def _get_nc():
    if "nc" not in _CACHED:
        nc, P = build_core(16)
        _CACHED["nc"] = nc
    return _CACHED["nc"]

def kernel(L0_attn_x, L0_attn_Wq, L0_attn_bq, L0_attn_Wdown, L0_attn_bdown,
           L0_attn_Wup, L0_attn_bup, L0_attn_Wout, L0_attn_bout,
           L0_ffn_gate_w, L0_ffn_expert_w):
    from concourse.bass_utils import run_bass_kernel_spmd
    x = np.ascontiguousarray(np.asarray(L0_attn_x, dtype=np.float32).reshape(65536, 1024))
    shared = {
        "Wq": np.ascontiguousarray(np.asarray(L0_attn_Wq, dtype=np.float32)),
        "bq": np.ascontiguousarray(np.asarray(L0_attn_bq, dtype=np.float32)),
        "Wdown": np.ascontiguousarray(np.asarray(L0_attn_Wdown, dtype=np.float32)),
        "bdown": np.ascontiguousarray(np.asarray(L0_attn_bdown, dtype=np.float32)),
        "Wup": np.ascontiguousarray(np.asarray(L0_attn_Wup, dtype=np.float32)),
        "bup": np.ascontiguousarray(np.asarray(L0_attn_bup, dtype=np.float32)),
        "Wout": np.ascontiguousarray(np.asarray(L0_attn_Wout, dtype=np.float32)),
        "bout": np.ascontiguousarray(np.asarray(L0_attn_bout, dtype=np.float32)),
        "gw": np.ascontiguousarray(np.asarray(L0_ffn_gate_w, dtype=np.float32)),
        "ew": np.ascontiguousarray(np.asarray(L0_ffn_expert_w, dtype=np.float32)),
    }
    in_maps = []
    for c in range(8):
        m = dict(shared)
        m["x"] = np.ascontiguousarray(x[c*8192:(c+1)*8192])
        in_maps.append(m)
    nc = _get_nc()
    res = run_bass_kernel_spmd(nc, in_maps, core_ids=list(range(8)))
    out = np.concatenate([res.results[c]["out"] for c in range(8)], axis=0)
    return out.reshape(16, 4096, 1024)
